# version 02
# baseline (speedup 1.0000x reference)
import numpy as np
import ml_dtypes
import concourse.bass as bass
import concourse.mybir as mybir
from concourse.bass_utils import run_bass_kernel_spmd

F32 = mybir.dt.float32
BF16 = mybir.dt.bfloat16
AF = mybir.ActivationFunctionType
ALU = mybir.AluOpType
AXX = mybir.AxisListType.X

HID = 2048
T = 2048
TOK = 1024
NH = 16
NE = 64
TOPK = 8
G = 4
KC = HID // 128
EPS = 1e-5
BIG = 1.0e9
INVH = 1.0 / HID
INVD = 1.0 / 128.0
QSC = 128.0 ** -0.5

SCORE_CHUNKS = []
_ti = 0
for _qc in range(2):
    for _kt in range(16):
        if _kt < 8:
            if _kt * 128 > _qc * 512 + 511:
                continue
            if _kt * 128 + 127 <= _qc * 512:
                SCORE_CHUNKS.append((_qc, _kt, None))
            else:
                SCORE_CHUNKS.append((_qc, _kt, _ti)); _ti += 1
        else:
            SCORE_CHUNKS.append((_qc, _kt, _ti)); _ti += 1
N_TAU = _ti
assert len(SCORE_CHUNKS) == 28 and N_TAU == 24


class Ctr:
    __slots__ = ("h", "name", "val")

    def __init__(self, h, name):
        self.h, self.name, self.val = h, name, 0


class Prog:

    def __init__(self, nc):
        self.nc = nc
        self.ev = {e: [] for e in ("sync", "gpsimd", "scalar", "vector", "tensor")}
        self.vctr = None

    def sem(self, name):
        return Ctr(self.nc.alloc_semaphore(name), name)

    def emit(self, eng, fn, wait=(), inc=()):
        wait = list(wait)
        inc = list(inc)
        if eng == "vector" and self.vctr is not None:
            wait.append((self.vctr, self.vctr.val))
            if not any(s is self.vctr for s, _ in inc):
                inc.append((self.vctr, 1))
        incs = []
        for s, n in inc:
            s.val += n
            incs.append((s, n))
        seen = {}
        for s, v in wait:
            if v > 0 and (id(s) not in seen or seen[id(s)][1] < v):
                seen[id(s)] = (s, v)
        self.ev[eng].append((fn, list(seen.values()), incs))

    def finish(self):
        with self.nc.Block() as blk:
            def mk(name):
                def body(eng):
                    for fn, waits, incs in self.ev[name]:
                        for s, v in waits:
                            eng.wait_ge(s.h, v)
                        last = fn(eng)
                        for s, n in incs:
                            last = last.then_inc(s.h, n)
                return body
            blk.sync(mk("sync"))
            blk.gpsimd(mk("gpsimd"))
            blk.scalar(mk("scalar"))
            blk.vector(mk("vector"))
            blk.tensor(mk("tensor"))


def build_program(mode="full", ne_run=NE, dumps=()):
    nc = bass.Bass("TRN2", target_bir_lowering=False, debug=False)
    p = Prog(nc)
    do_attn = mode in ("full", "attn")
    do_moe = mode in ("full", "moe")
    dram = {}

    def din(name, shape, dtype=F32):
        dram[name] = nc.dram_tensor(name, list(shape), dtype, kind="ExternalInput").ap()
        return dram[name]

    def dout(name, shape, dtype=F32):
        dram[name] = nc.dram_tensor(name, list(shape), dtype, kind="ExternalOutput").ap()
        return dram[name]

    if do_attn:
        d_xseq = din("x_seq", (T, HID))
        d_wq = din("wq", (NH, 128, KC, 128))
        d_wk = din("wk", (NH, 128, KC, 128))
        d_wv = din("wv", (NH, 128, KC, 128))
        d_wo = din("wo", (128, KC, HID))
        d_gam = din("gamma", (128, 1))
        d_bet = din("beta", (128, 1))
        d_tau = din("tau", (128, N_TAU))
        d_xown = din("x_own", (TOK, HID))
    if mode == "moe":
        d_x2in = din("x2_in", (TOK, HID))
    if do_moe:
        d_gu = din("gu", (NE, 8, 128, KC, 128))
        d_dp = din("dp", (KC, 128, NE, 4, 128))
        d_gatew = din("gatew", (128, KC, NE))
        d_shg = din("shg", (2, 4, 128, KC, 128))
        d_shd = din("shd", (4, 128, 4, 4, 128))
        d_shg1 = din("shg1", (128, KC, 1))
    d_cb = din("consts_b", (128, 256), BF16)
    d_cf = din("consts_f", (128, 772), F32)
    d_out = dout("x2_out" if mode == "attn" else "out", (TOK, HID))

    cb = nc.alloc_sbuf_tensor("cb", [128, 256], BF16)
    cf = nc.alloc_sbuf_tensor("cf", [128, 772], F32)
    ident_b, ones_col_b = cb[:, 0:128], cb[:, 128:129]
    ones_row_b = cb[0:1, 128:256]
    ident_f, ones_col_f = cf[:, 0:128], cf[:, 128:129]
    ones_row_f = cf[0:1, 128:256]
    iota_fp = cf[:, 256:768]
    epsv = cf[:, 768:769]
    epsr = cf[0:1, 768:769]
    scal = nc.alloc_sbuf_tensor("scal", [128, 16], F32)
    if do_attn:
        gam = nc.alloc_sbuf_tensor("gam", [128, 1], F32)
        bet = nc.alloc_sbuf_tensor("bet", [128, 1], F32)
        taus = nc.alloc_sbuf_tensor("taus", [128, N_TAU], F32)
    if do_moe:
        gwb = nc.alloc_sbuf_tensor("gwb", [128, KC * NE], BF16)
        gwv = gwb[:].rearrange("p (c n) -> p c n", c=KC)
        shg1b = nc.alloc_sbuf_tensor("shg1b", [128, KC], BF16)
        ltile = nc.alloc_sbuf_tensor("ltile", [128, NE], F32)
        cur = nc.alloc_sbuf_tensor("cur", [128, NE], F32)
        mskt = nc.alloc_sbuf_tensor("mskt", [128, NE], F32)
        expv = nc.alloc_sbuf_tensor("expv", [128, NE], F32)
        mx = nc.alloc_sbuf_tensor("mx", [128, TOPK], F32)
        wtok = nc.alloc_sbuf_tensor("wtok", [128, NE], F32)
        wfeat = nc.alloc_sbuf_tensor("wfeat", [64, TOK], F32)
        wrow = nc.alloc_sbuf_tensor("wrow", [1, TOK], BF16)
        g1r = nc.alloc_sbuf_tensor("g1r", [1, TOK], F32)
        g1b = nc.alloc_sbuf_tensor("g1b", [128, TOK], F32)
        oinsp = nc.alloc_sbuf_tensor("oinsp", [128, 512], F32)


    SB_END = 16512 + 212832
    _free0 = nc.sbuf_bytes_remaining
    BA = (SB_END - _free0 + 127) // 128 * 128

    def at(name, shape, dtype, off):
        nb = int(np.prod(shape[1:])) * mybir.dt.size(dtype)
        assert BA + off + nb <= SB_END, f"{name}: {BA + off + nb} > {SB_END}"
        return nc.alloc_sbuf_tensor_at(name, list(shape), dtype, offset=BA + off)

    if do_attn:
        r1 = at("r1", (128, KC * T), BF16, 0)
        r1v = r1[:].rearrange("p (c t) -> p c t", c=KC)
    x2 = at("x2", (128, 8 * HID), F32, 0)
    if do_moe:
        inter = at("inter", (128, G * 4 * TOK), BF16, 0)
        interv = inter[:].rearrange("p (i t) -> p i t", i=G * 4)
        sgt = at("sgt", (128, 2 * 512), BF16, 32768)
        itt = at("itt", (128, 2 * 512), BF16, 32768 + 2048)
        exsp = at("exsp", (128, 512), F32, 32768 + 4096)
    RB = 65536
    if do_attn:
        xin = at("xin", (128, 3 * HID), F32, RB)
        sqb_l1 = at("sqb_l1", (128, HID), BF16, RB + 24576)
        ofeat = at("ofeat", (128, KC * TOK), BF16, RB)
        ofv = ofeat[:].rearrange("p (c t) -> p c t", c=KC)
        rowf = at("rowf", (1, 8 * 512), F32, RB + 32768)
        rowb = at("rowb", (1, 4 * 512), BF16, RB + 32768 + 16384)
        y2scr = at("y2scr", (128, 512), BF16, RB + 32768 + 16384 + 4096)
        wob = at("wob", (128, 2 * KC * 512), BF16, RB + 32768)
        wobv = [wob[:, i * KC * 512:(i + 1) * KC * 512].rearrange(
            "p (c n) -> p c n", c=KC) for i in range(2)]
    if do_moe:
        sqb = at("sqb", (128, HID), BF16, RB + 32768)
        oacc = at("oacc", (128, KC * TOK), F32, RB)
        oav = oacc[:].rearrange("p (c t) -> p c t", c=KC)
    RC = 131072
    if do_attn:
        y0k = at("y0k", (128, T), BF16, RC)
        vft = at("vft", (128, T), BF16, RC + 4096)
        kfm = at("kfm", (128, T), BF16, RC + 8192)
        qfm = at("qfm", (128, TOK), BF16, RC + 12288)
        y0q = at("y0q", (128, TOK), BF16, RC + 14336)
        vtok = at("vtok", (128, 16 * 128), BF16, RC + 16384)
        ssq = at("ssq", (128, 4 * 512), BF16, RC + 20480)
        zbs = at("zbs", (128, 512), F32, RC + 24576)
        nrmt = at("nrmt", (128, 512), BF16, RC + 26624)
        zrow = at("zrow", (1, 512), F32, RC + 28672)
    if do_moe:
        r2 = at("r2", (128, KC * TOK), BF16, RC)
        r2v = r2[:].rearrange("p (c t) -> p c t", c=KC)
    RE = 163840
    if do_attn:
        wqkv = at("wqkv", (128, 2 * 3 * KC * 128), BF16, RE)
        wvv = wqkv[:].rearrange("p (s w c j) -> p s w c j", s=2, w=3, c=KC)
        xores = at("xores", (128, 2 * 512), F32, RE)
    if do_moe:
        gur = at("gur", (128, 3 * KC * 128), BF16, RE)
        gurv = gur[:].rearrange("p (s c j) -> p s c j", s=3, c=KC)
        dpr = at("dpr", (128, 2 * 16 * 128), BF16, RE + 12288)
        dprv = dpr[:].rearrange("p (s e i j) -> p s e i j", s=2, e=G, i=4)
        dprs = dpr[:].rearrange("p (s h i j) -> p s h i j", s=2, h=4, i=4)
        shsg = at("shsg", (128, 4 * 512), BF16, RE + 12288 + 8192)
        shsv = shsg[:].rearrange("p (m t) -> p m t", m=4)
        wbc = at("wbc", (128, 2 * TOK), BF16, RE + 12288 + 8192)
    outstg = at("outstg", (128, 2 * HID), F32, RE)

    pf = [nc.alloc_psum_tensor(f"pf{i}", [128, 512], F32) for i in range(7)]
    pbf = nc.alloc_psum_tensor("pbf", [128, 1024], BF16)

    s_pe = p.sem("s_pe")
    s_act = p.sem("s_act")
    s_dve = p.sem("s_dve")
    p.vctr = s_dve
    s_wqkv = [p.sem("s_wqkv0"), p.sem("s_wqkv1")]
    s_wo = [p.sem("s_wo0"), p.sem("s_wo1")]
    s_gu = [p.sem(f"s_gu{i}") for i in range(3)]
    s_dp = [p.sem(f"s_dp{i}") for i in range(2)]
    s_one = p.sem("s_one")
    s_x = p.sem("s_x")
    s_st = p.sem("s_st")
    s_wr = p.sem("s_wr")

    chk = {}

    def mark(key, ctr):
        chk[key] = (ctr, ctr.val)

    def W(*keys):
        return [chk[k] for k in keys if k in chk]

    def now(*ctrs):
        return [(c, c.val) for c in ctrs]

    def soft():
        return now(s_pe, s_act, s_dve)

    p.emit("sync", lambda e: e.dma_start(out=cb[:], in_=d_cb), inc=[(s_x, 16)])
    p.emit("sync", lambda e: e.dma_start(out=cf[:], in_=d_cf), inc=[(s_x, 16)])
    if do_attn:
        p.emit("sync", lambda e: e.dma_start(out=gam[:], in_=d_gam), inc=[(s_x, 16)])
        p.emit("sync", lambda e: e.dma_start(out=bet[:], in_=d_bet), inc=[(s_x, 16)])
        p.emit("sync", lambda e: e.dma_start(out=taus[:], in_=d_tau), inc=[(s_x, 16)])
    if do_moe:
        p.emit("gpsimd", lambda e: e.dma_start(out=gwv, in_=d_gatew), inc=[(s_one, 16)])
        p.emit("gpsimd", lambda e: e.dma_start(
            out=shg1b[:].rearrange("p (c n) -> p c n", n=1), in_=d_shg1),
            inc=[(s_one, 16)])
    base_w = now(s_x, s_one)

    def ln_tile(x_ap, out_ap, extra_wait=()):
        p.emit("vector", lambda e: e.reduce_sum(scal[:, 0:1], x_ap, axis=AXX),
               wait=tuple(extra_wait), inc=[(s_dve, 1)])
        def sqf(e):
            last = None
            for cc in range(4):
                last = e.activation(out=pf[0][:], in_=x_ap[:, cc * 512:(cc + 1) * 512],
                                    func=AF.Square, accum_out=scal[:, 4 + cc:5 + cc])
            return last
        p.emit("scalar", sqf, wait=tuple(extra_wait) + tuple(now(s_dve, s_pe)),
               inc=[(s_act, 1)])
        def mid1(e):
            e.reduce_sum(scal[:, 1:2], scal[:, 4:8], axis=AXX)
            return e.tensor_scalar(out=scal[:, 0:1], in0=scal[:, 0:1], scalar1=INVH,
                                   scalar2=None, op0=ALU.mult)
        p.emit("vector", mid1, wait=now(s_act))
        def mid2(e):
            e.tensor_scalar(out=scal[:, 1:2], in0=scal[:, 1:2], scalar1=INVH,
                            scalar2=None, op0=ALU.mult)
            return e.tensor_tensor(out=scal[:, 2:3], in0=scal[:, 0:1],
                                   in1=scal[:, 0:1], op=ALU.mult)
        p.emit("vector", mid2)
        p.emit("vector", lambda e: e.tensor_tensor(out=scal[:, 1:2], in0=scal[:, 1:2],
                                                   in1=scal[:, 2:3], op=ALU.subtract))
        p.emit("scalar", lambda e: e.activation(out=scal[:, 3:4], in_=scal[:, 1:2],
                                                func=AF.Sqrt, bias=epsv),
               wait=now(s_dve), inc=[(s_act, 1)])
        p.emit("vector", lambda e: e.reciprocal(out=scal[:, 3:4], in_=scal[:, 3:4]),
               wait=now(s_act))
        p.emit("vector", lambda e: e.tensor_scalar(
            out=out_ap, in0=x_ap, scalar1=scal[:, 0:1], scalar2=scal[:, 3:4],
            op0=ALU.subtract, op1=ALU.mult))

    def transpose_tile(src_ap, dst_fn, nchunk):
        for j0 in range(0, nchunk, 8):
            jn = min(8, nchunk - j0)
            def tr(e, j0=j0, jn=jn):
                last = None
                for j in range(jn):
                    last = e.transpose(pbf[:, j * 128:(j + 1) * 128],
                                       src_ap[:, (j0 + j) * 128:(j0 + j + 1) * 128],
                                       ident_b)
                return last
            p.emit("tensor", tr, wait=now(s_dve), inc=[(s_pe, 1)])
            def ev(e, j0=j0, jn=jn):
                last = None
                for j in range(jn):
                    last = e.tensor_copy(out=dst_fn(j0 + j),
                                         in_=pbf[:, j * 128:(j + 1) * 128])
                return last
            p.emit("vector", ev, wait=now(s_pe), inc=[(s_dve, 1)])

    if do_attn:
        for t in range(16):
            slot = t % 3
            xt = xin[:, slot * HID:(slot + 1) * HID]
            p.emit("sync", lambda e, xt=xt, t=t: e.dma_start(
                out=xt, in_=d_xseq[t * 128:(t + 1) * 128, :]),
                wait=W(f"l1fin_{t - 3}"), inc=[(s_x, 16)])
            ln_tile(xt, sqb_l1[:], extra_wait=now(s_x) + (base_w if t == 0 else []))
            mark(f"l1fin_{t}", s_dve)
            transpose_tile(sqb_l1[:], lambda j, t=t: r1v[:, j, t * 128:(t + 1) * 128], KC)
        mark("r1_done", s_dve)

        for h in range(NH):
            sl = h % 2
            for wi, dsrc in enumerate((d_wq, d_wk, d_wv)):
                p.emit("gpsimd", lambda e, wi=wi, dsrc=dsrc, sl=sl, h=h: e.dma_start(
                    out=wvv[:, sl, wi, :, :], in_=dsrc[h]),
                    wait=W(f"hproj_{h - 2}") if wi == 0 else [],
                    inc=[(s_wqkv[sl], 16)])
            ldw = now(s_wqkv[sl])

            def proj(dst_fn, wsel, nck, fm, first):
                for q2 in range(nck):
                    b = q2 % 2
                    wv_ = list(ldw) if q2 == 0 else []
                    if first and q2 == 0:
                        wv_ += W("r1_done") + now(s_act)
                    wv_ += W(f"hpb_{b}")
                    def mm(e, q2=q2, b=b, wsel=wsel, sl=sl):
                        last = None
                        for c in range(KC):
                            last = e.matmul(pf[b][:], wvv[:, sl, wsel, c, :],
                                            r1v[:, c, q2 * 512:(q2 + 1) * 512],
                                            start=(c == 0), stop=(c == KC - 1))
                        return last
                    p.emit("tensor", mm, wait=wv_, inc=[(s_pe, 1)])
                    def evac(e, q2=q2, b=b, dst_fn=dst_fn, fm=fm):
                        dst = dst_fn(q2)
                        if fm:
                            return e.tensor_scalar(out=dst, in0=pf[b][:], scalar1=gam[:],
                                                   scalar2=bet[:], op0=ALU.mult,
                                                   op1=ALU.add)
                        return e.tensor_copy(out=dst, in_=pf[b][:])
                    p.emit("vector", evac, wait=now(s_pe), inc=[(s_dve, 1)])
                    mark(f"hpb_{b}", s_dve)
            proj(lambda q2: y0q[:, q2 * 512:(q2 + 1) * 512], 0, 2, True, h == 0)
            proj(lambda q2: y0k[:, q2 * 512:(q2 + 1) * 512], 1, 4, True, False)
            proj(lambda q2: vft[:, q2 * 512:(q2 + 1) * 512], 2, 4, False, False)
            mark(f"hproj_{h}", s_pe)

            transpose_tile(vft[:], lambda j: vtok[:, j * 128:(j + 1) * 128], 16)

            def fmnorm(y0, dst, nck, scale):
                for q2 in range(nck):
                    ysl = y0[:, q2 * 512:(q2 + 1) * 512]
                    p.emit("tensor", lambda e, ysl=ysl: e.matmul(
                        pf[6][0:1, :], ones_col_b, ysl, start=True, stop=True),
                        wait=now(s_dve), inc=[(s_pe, 1)])
                    p.emit("vector", lambda e: e.tensor_scalar(
                        out=rowf[0:1, 0:512], in0=pf[6][0:1, :], scalar1=INVD,
                        scalar2=None, op0=ALU.mult), wait=now(s_pe), inc=[(s_dve, 1)])
                    p.emit("scalar", lambda e, ysl=ysl: e.activation(
                        out=y2scr[:], in_=ysl, func=AF.Square),
                        wait=now(s_pe), inc=[(s_act, 1)])
                    p.emit("tensor", lambda e: e.matmul(
                        pf[6][0:1, :], ones_col_b, y2scr[:], start=True, stop=True),
                        wait=now(s_act, s_dve), inc=[(s_pe, 1)])
                    p.emit("vector", lambda e: e.tensor_tensor(
                        out=rowf[0:1, 512:1024], in0=rowf[0:1, 0:512],
                        in1=rowf[0:1, 0:512], op=ALU.mult), wait=now(s_pe))
                    p.emit("vector", lambda e: e.scalar_tensor_tensor(
                        out=rowf[0:1, 1024:1536], in0=pf[6][0:1, :], scalar=INVD,
                        in1=rowf[0:1, 512:1024], op0=ALU.mult, op1=ALU.subtract))
                    p.emit("scalar", lambda e: e.activation(
                        out=rowf[0:1, 1536:2048], in_=rowf[0:1, 1024:1536],
                        func=AF.Sqrt, bias=epsr), wait=now(s_dve), inc=[(s_act, 1)])
                    p.emit("vector", lambda e: e.reciprocal(
                        out=rowf[0:1, 1536:2048], in_=rowf[0:1, 1536:2048]),
                        wait=now(s_act))
                    if scale != 1.0:
                        p.emit("vector", lambda e, scale=scale: e.tensor_scalar(
                            out=rowf[0:1, 1536:2048], in0=rowf[0:1, 1536:2048],
                            scalar1=scale, scalar2=None, op0=ALU.mult))
                    p.emit("vector", lambda e: e.scalar_tensor_tensor(
                        out=rowf[0:1, 512:1024], in0=rowf[0:1, 0:512], scalar=-1.0,
                        in1=rowf[0:1, 1536:2048], op0=ALU.mult, op1=ALU.mult))
                    def abcp(e):
                        e.tensor_copy(out=rowb[0:1, 0:512], in_=rowf[0:1, 1536:2048])
                        return e.tensor_copy(out=rowb[0:1, 512:1024],
                                             in_=rowf[0:1, 512:1024])
                    p.emit("vector", abcp)
                    p.emit("tensor", lambda e: e.matmul(
                        pf[2][:], ones_row_b, rowb[0:1, 0:512], start=True, stop=True),
                        wait=now(s_dve), inc=[(s_pe, 1)])
                    p.emit("tensor", lambda e: e.matmul(
                        pf[3][:], ones_row_b, rowb[0:1, 512:1024], start=True,
                        stop=True), inc=[(s_pe, 1)])
                    def nrm(e, ysl=ysl, q2=q2, dst=dst):
                        e.tensor_tensor(out=nrmt[:], in0=ysl, in1=pf[2][:], op=ALU.mult)
                        return e.tensor_tensor(out=dst[:, q2 * 512:(q2 + 1) * 512],
                                               in0=nrmt[:], in1=pf[3][:], op=ALU.add)
                    p.emit("vector", nrm, wait=now(s_pe), inc=[(s_dve, 1)])
            fmnorm(y0q, qfm, 2, QSC)
            fmnorm(y0k, kfm, 4, 1.0)

            for q2 in range(2):
                lst = [(qc, kt, ti) for (qc, kt, ti) in SCORE_CHUNKS if qc == q2]
                nl = len(lst)
                def emit_score(ci):
                    qc, kt, ti = lst[ci]
                    sb_ = ci % 2
                    ssl = ssq[:, (ci % 4) * 512:(ci % 4 + 1) * 512]
                    p.emit("tensor", lambda e, kt=kt, qc=qc, sb_=sb_: e.matmul(
                        pf[2 + sb_][:], kfm[:, kt * 128:(kt + 1) * 128],
                        qfm[:, qc * 512:(qc + 1) * 512], start=True, stop=True),
                        wait=W(f"sqev_{sb_}") + (now(s_dve) if ci == 0 else []),
                        inc=[(s_pe, 1)])
                    p.emit("scalar", lambda e, sb_=sb_, ssl=ssl: e.activation(
                        out=ssl, in_=pf[2 + sb_][:], func=AF.Square),
                        wait=now(s_pe) + W(f"avz_{ci % 4}"), inc=[(s_act, 1)])
                    mark(f"sqev_{sb_}", s_act)
                    if ti is not None:
                        p.emit("vector", lambda e, ssl=ssl, ti=ti:
                               e.scalar_tensor_tensor(out=ssl, in0=iota_fp,
                                                      scalar=taus[:, ti:ti + 1],
                                                      in1=ssl, op0=ALU.is_ge,
                                                      op1=ALU.mult),
                               wait=now(s_act), inc=[(s_dve, 1)])
                    mark(f"smask_{ci % 4}", s_dve if ti is not None else s_act)
                def emit_avz(ci):
                    qc, kt, ti = lst[ci]
                    ssl = ssq[:, (ci % 4) * 512:(ci % 4 + 1) * 512]
                    mk = chk.get(f"smask_{ci % 4}")
                    def avz(e, kt=kt, q2=q2, ssl=ssl, first=(ci == 0),
                            last_=(ci == nl - 1)):
                        e.matmul(pf[4 + q2][:], vtok[:, kt * 128:(kt + 1) * 128],
                                 ssl, start=first, stop=last_, skip_group_check=True)
                        return e.matmul(pf[6][0:1, :], ones_col_b, ssl,
                                        start=first, stop=last_, skip_group_check=True)
                    wv_ = [mk] if mk else []
                    if ci == 0:
                        wv_ += W(f"oev_{q2}") + W("zread")
                    p.emit("tensor", avz, wait=wv_, inc=[(s_pe, 1)])
                    mark(f"avz_{ci % 4}", s_pe)
                for ci in range(nl):
                    emit_score(ci)
                    if ci >= 2:
                        emit_avz(ci - 2)
                emit_avz(nl - 2)
                emit_avz(nl - 1)
                p.emit("vector", lambda e: e.tensor_scalar(
                    out=zrow[0:1, :], in0=pf[6][0:1, :], scalar1=EPS,
                    scalar2=None, op0=ALU.add), wait=now(s_pe))
                mark("zread", s_dve)
                p.emit("vector", lambda e: e.reciprocal(out=zrow[0:1, :],
                                                        in_=zrow[0:1, :]))
                p.emit("tensor", lambda e: e.matmul(
                    pf[2][:], ones_row_f, zrow[0:1, :], start=True, stop=True),
                    wait=now(s_dve, s_act), inc=[(s_pe, 1)])
                p.emit("scalar", lambda e: e.copy(out=zbs[:], in_=pf[2][:]),
                       wait=now(s_pe), inc=[(s_act, 1)])
                p.emit("vector", lambda e, h=h, q2=q2: e.tensor_tensor(
                    out=ofv[:, h, q2 * 512:(q2 + 1) * 512], in0=pf[4 + q2][:],
                    in1=zbs[:], op=ALU.mult), wait=now(s_act), inc=[(s_dve, 1)])
                mark(f"oev_{q2}", s_dve)
            mark(f"hdone_{h}", s_dve)
        mark(f"hproj_{NH - 1}", s_pe)

        for n in range(4):
            wsl = n % 2
            p.emit("gpsimd", lambda e, n=n, wsl=wsl: e.dma_start(
                out=wobv[wsl], in_=d_wo[:, :, n * 512:(n + 1) * 512]),
                wait=(W(f"hdone_{NH - 1}") if n == 0 else W(f"wopg_{n - 2}")),
                inc=[(s_wo[wsl], 16)])
            ldw = now(s_wo[wsl])
            for m in range(8):
                xsl = m % 2
                p.emit("sync", lambda e, n=n, m=m, xsl=xsl: e.dma_start(
                    out=xores[:, xsl * 512:(xsl + 1) * 512],
                    in_=d_xown[m * 128:(m + 1) * 128, n * 512:(n + 1) * 512]),
                    wait=(W(f"hproj_{NH - 1}") if (n == 0 and m == 0) else []) +
                         W(f"xores_{xsl}"), inc=[(s_x, 16)])
                ldx = now(s_x)
                b = m % 2
                wv_ = (list(ldw) if m == 0 else []) + W(f"wob_{b}")
                def wmm(e, m=m, wsl=wsl, b=b):
                    last = None
                    for c in range(KC):
                        last = e.matmul(pf[b][:], ofv[:, c, m * 128:(m + 1) * 128],
                                        wobv[wsl][:, c, :], start=(c == 0),
                                        stop=(c == KC - 1))
                    return last
                p.emit("tensor", wmm, wait=wv_, inc=[(s_pe, 1)])
                p.emit("vector", lambda e, n=n, m=m, b=b, xsl=xsl: e.tensor_tensor(
                    out=x2[:, m * HID + n * 512: m * HID + (n + 1) * 512],
                    in0=pf[b][:], in1=xores[:, xsl * 512:(xsl + 1) * 512], op=ALU.add),
                    wait=now(s_pe) + ldx, inc=[(s_dve, 1)])
                mark(f"wob_{b}", s_dve)
                mark(f"xores_{xsl}", s_dve)
            mark(f"wopg_{n}", s_pe)
        mark("x2_done", s_dve)

        if mode == "attn":
            if dumps:
                dbg = {"r1": r1, "y0q": y0q, "y0k": y0k, "qfm": qfm, "kfm": kfm,
                       "vtok": vtok, "ofeat": ofeat, "zbs": zbs, "vft": vft}
                for name in dumps:
                    tsr = dbg[name]
                    shp = [int(xx) for xx in tsr[:].shape]
                    od = nc.dram_tensor(f"dump_{name}", shp, tsr[:].dtype,
                                        kind="ExternalOutput").ap()
                    p.emit("sync", lambda e, od=od, tsr=tsr: e.dma_start(
                        out=od, in_=tsr[:]),
                        wait=W("x2_done") + now(s_dve, s_act, s_pe), inc=[(s_st, 16)])
            for t in range(8):
                p.emit("sync", lambda e, t=t: e.dma_start(
                    out=d_out[t * 128:(t + 1) * 128, :],
                    in_=x2[:, t * HID:(t + 1) * HID]),
                    wait=W("x2_done"), inc=[(s_st, 16)])
            p.emit("sync", lambda e, v=s_st.val: e.wait_ge(s_st.h, v))
            return nc, p, dram

    if mode == "moe":
        for t in range(8):
            p.emit("sync", lambda e, t=t: e.dma_start(
                out=x2[:, t * HID:(t + 1) * HID],
                in_=d_x2in[t * 128:(t + 1) * 128, :]), inc=[(s_x, 16)])
        p.emit("vector", lambda e: e.engine_nop(),
               wait=now(s_x) + base_w, inc=[(s_dve, 1)])
        mark("x2_done", s_dve)

    for t in range(8):
        ln_tile(x2[:, t * HID:(t + 1) * HID], sqb[:],
                extra_wait=(W("x2_done") + soft()) if t == 0 else ())
        transpose_tile(sqb[:], lambda j, t=t: r2v[:, j, t * 128:(t + 1) * 128], KC)
    mark("r2_done", s_dve)

    for tc in range(2):
        def g1mm(e, tc=tc):
            last = None
            for c in range(KC):
                last = e.matmul(pf[6][0:1, :], shg1b[:, c:c + 1],
                                r2v[:, c, tc * 512:(tc + 1) * 512],
                                start=(c == 0), stop=(c == KC - 1))
            return last
        p.emit("tensor", g1mm, wait=now(s_dve, s_act), inc=[(s_pe, 1)])
        p.emit("scalar", lambda e, tc=tc: e.activation(
            out=g1r[0:1, tc * 512:(tc + 1) * 512], in_=pf[6][0:1, :], func=AF.Sigmoid),
            wait=now(s_pe), inc=[(s_act, 1)])
        p.emit("tensor", lambda e, tc=tc: e.matmul(
            pf[6][:], ones_row_f, g1r[0:1, tc * 512:(tc + 1) * 512],
            start=True, stop=True), wait=now(s_act), inc=[(s_pe, 1)])
        p.emit("scalar", lambda e, tc=tc: e.copy(
            out=g1b[:, tc * 512:(tc + 1) * 512], in_=pf[6][:]),
            wait=now(s_pe), inc=[(s_act, 1)])
    mark("g1_done", s_act)

    for tc in range(2):
        for gi in range(2):
            for mc in range(4):
                ci = tc * 8 + gi * 4 + mc
                slot = ci % 2
                p.emit("gpsimd", lambda e, gi=gi, mc=mc, slot=slot: e.dma_start(
                    out=gurv[:, slot, :, :], in_=d_shg[gi, mc]),
                    wait=(W("x2_done") if ci == 0 else W(f"gusl_{slot}")),
                    inc=[(s_gu[slot], 16)])
                ldw = now(s_gu[slot])
                b = mc % 2 if gi == 0 else 2 + mc % 2
                wv_ = list(ldw) + W(f"shb_{b}")
                def smm(e, slot=slot, tc=tc, b=b):
                    last = None
                    for c in range(KC):
                        last = e.matmul(pf[b][:], gurv[:, slot, c, :],
                                        r2v[:, c, tc * 512:(tc + 1) * 512],
                                        start=(c == 0), stop=(c == KC - 1))
                    return last
                p.emit("tensor", smm, wait=wv_, inc=[(s_pe, 1)])
                mark(f"gusl_{slot}", s_pe)
                if gi == 0:
                    p.emit("scalar", lambda e, mc=mc, b=b: e.activation(
                        out=shsv[:, mc, :], in_=pf[b][:], func=AF.Silu),
                        wait=now(s_pe) + W(f"shsl_{mc}"), inc=[(s_act, 1)])
                    mark(f"shb_{b}", s_act)
                else:
                    p.emit("vector", lambda e, mc=mc, b=b: e.tensor_tensor(
                        out=shsv[:, mc, :], in0=pf[b][:],
                        in1=shsv[:, mc, :], op=ALU.mult),
                        wait=now(s_pe, s_act), inc=[(s_dve, 1)])
                    mark(f"shb_{b}", s_dve)
        for l4 in range(4):
            slot = l4 % 2
            p.emit("gpsimd", lambda e, l4=l4, slot=slot: e.dma_start(
                out=dprs[:, slot], in_=d_shd[l4]),
                wait=W(f"dpsl_{slot}"), inc=[(s_dp[slot], 16)])
            ldw = now(s_dp[slot])
            for hl in range(4):
                hm = l4 * 4 + hl
                b = hl % 2
                wv_ = (list(ldw) if hl == 0 else []) + W(f"sdb_{b}") + \
                      (now(s_dve, s_act) if l4 == 0 and hl == 0 else [])
                def dmm(e, slot=slot, hl=hl, tc=tc, b=b):
                    last = None
                    for ic in range(4):
                        last = e.matmul(pf[b][:], dprs[:, slot, hl, ic, :],
                                        shsv[:, ic, :],
                                        start=(ic == 0), stop=(ic == 3))
                    return last
                p.emit("tensor", dmm, wait=wv_, inc=[(s_pe, 1)])
                def xtr(e, hm=hm, tc=tc, b=b):
                    last = None
                    for u in range(4):
                        tt = tc * 4 + u
                        last = e.transpose(
                            pf[2 + b][:, u * 128:(u + 1) * 128],
                            x2[:, tt * HID + hm * 128: tt * HID + (hm + 1) * 128],
                            ident_f)
                    return last
                p.emit("tensor", xtr, wait=W(f"sdt_{b}"), inc=[(s_pe, 1)])
                def oinit(e, hm=hm, tc=tc, b=b):
                    e.tensor_tensor(out=oinsp[:], in0=pf[b][:],
                                    in1=g1b[:, tc * 512:(tc + 1) * 512], op=ALU.mult)
                    return e.tensor_tensor(out=oav[:, hm, tc * 512:(tc + 1) * 512],
                                           in0=pf[2 + b][:], in1=oinsp[:], op=ALU.add)
                p.emit("vector", oinit, wait=now(s_pe) + W("g1_done"),
                       inc=[(s_dve, 1)])
                mark(f"sdb_{b}", s_dve)
                mark(f"sdt_{b}", s_dve)
            mark(f"dpsl_{slot}", s_pe)
        for mc in range(4):
            mark(f"shsl_{mc}", s_pe)
    mark("oacc_init", s_dve)

    for m in range(8):
        b = m % 2
        wv_ = W(f"route_{b}") + (soft() if m == 0 else []) + \
              (now(s_one) if m == 0 else [])
        def lmm(e, m=m, b=b):
            last = None
            for c in range(KC):
                last = e.matmul(pf[b][:, 0:NE], r2v[:, c, m * 128:(m + 1) * 128],
                                gwv[:, c, :], start=(c == 0), stop=(c == KC - 1))
            return last
        p.emit("tensor", lmm, wait=wv_, inc=[(s_pe, 1)])
        def lcp(e, b=b):
            e.tensor_copy(out=ltile[:], in_=pf[b][:, 0:NE])
            return e.tensor_copy(out=cur[:], in_=pf[b][:, 0:NE])
        p.emit("vector", lcp, wait=now(s_pe, s_act))
        for r in range(TOPK):
            p.emit("vector", lambda e, r=r: e.reduce_max(mx[:, r:r + 1], cur[:],
                                                         axis=AXX))
            if r < TOPK - 1:
                p.emit("vector", lambda e, r=r: e.tensor_single_scalar(
                    out=mskt[:], in_=cur[:], scalar=mx[:, r:r + 1], op=ALU.is_ge))
                p.emit("vector", lambda e: e.scalar_tensor_tensor(
                    out=cur[:], in0=mskt[:], scalar=-BIG, in1=cur[:],
                    op0=ALU.mult, op1=ALU.add))
        p.emit("vector", lambda e: e.tensor_scalar(
            out=scal[:, 9:10], in0=mx[:, 0:1], scalar1=-1.0, scalar2=None,
            op0=ALU.mult))
        mark(f"route_{b}", s_dve)
        p.emit("scalar", lambda e: e.activation(out=expv[:], in_=ltile[:], func=AF.Exp,
                                                bias=scal[:, 9:10]),
               wait=now(s_dve), inc=[(s_act, 1)])
        p.emit("vector", lambda e: e.tensor_single_scalar(
            out=mskt[:], in_=ltile[:], scalar=mx[:, TOPK - 1:TOPK], op=ALU.is_ge),
            wait=now(s_act))
        p.emit("vector", lambda e: e.tensor_tensor(out=expv[:], in0=expv[:],
                                                   in1=mskt[:], op=ALU.mult))
        p.emit("vector", lambda e: e.reduce_sum(scal[:, 10:11], expv[:], axis=AXX))
        p.emit("vector", lambda e: e.reciprocal(out=scal[:, 10:11],
                                                in_=scal[:, 10:11]))
        p.emit("vector", lambda e: e.tensor_scalar(
            out=wtok[:], in0=expv[:], scalar1=scal[:, 10:11], scalar2=None,
            op0=ALU.mult))
        p.emit("tensor", lambda e: e.transpose(pf[6][0:NE, 0:128], wtok[:], ident_f),
               wait=now(s_dve), inc=[(s_pe, 1)])
        p.emit("vector", lambda e, m=m: e.tensor_copy(
            out=wfeat[0:NE, m * 128:(m + 1) * 128], in_=pf[6][0:NE, 0:128]),
            wait=now(s_pe), inc=[(s_dve, 1)])
    mark("wfeat_done", s_dve)

    for e_ in range(ne_run):
        g = e_ // G
        eg = e_ % G
        wsl = e_ % 2
        p.emit("gpsimd", lambda e, e_=e_: e.dma_start(
            out=wrow[0:1, :], in_=wfeat[e_:e_ + 1, :]),
            wait=W("wfeat_done") + W(f"wk1_{e_ - 1}"), inc=[(s_wr, 16)])
        ldr = now(s_wr)
        for half in range(2):
            wv_ = (list(ldr) if half == 0 else []) + W("wbc_evac")
            p.emit("tensor", lambda e, half=half: e.matmul(
                pf[6][:], ones_row_b,
                wrow[0:1, half * 512:(half + 1) * 512],
                start=True, stop=True), wait=wv_, inc=[(s_pe, 1)])
            p.emit("scalar", lambda e, wsl=wsl, half=half: e.copy(
                out=wbc[:, wsl * TOK + half * 512: wsl * TOK + (half + 1) * 512],
                in_=pf[6][:]), wait=now(s_pe) +
                (W("shint_last") if e_ == 0 and half == 0 else []),
                inc=[(s_act, 1)])
            mark("wbc_evac", s_act)
        mark(f"wk1_{e_}", s_pe)
        if e_ == 0:
            pass

        for ps in range(4):
            for gi in range(2):
                m_ = ps + 4 * gi
                ci = e_ * 8 + ps * 2 + gi
                slot = ci % 3
                p.emit("gpsimd", lambda e, e_=e_, m_=m_, slot=slot: e.dma_start(
                    out=gurv[:, slot, :, :], in_=d_gu[e_, m_]),
                    wait=W(f"gusl_{slot}"), inc=[(s_gu[slot], 16)])
            for tc in range(2):
                cg = e_ * 8 + ps * 2
                sg_, su_ = cg % 3, (cg + 1) % 3
                for gi2, b, sl_ in ((0, tc, sg_), (1, 2 + tc, su_)):
                    wv_ = (now(s_gu[sl_]) if tc == 0 else []) + W(f"exb_{b}")
                    def gmm(e, sl_=sl_, tc=tc, b=b):
                        last = None
                        for c in range(KC):
                            last = e.matmul(pf[b][:], gurv[:, sl_, c, :],
                                            r2v[:, c, tc * 512:(tc + 1) * 512],
                                            start=(c == 0), stop=(c == KC - 1))
                        return last
                    p.emit("tensor", gmm, wait=wv_, inc=[(s_pe, 1)])
                mark(f"gusl_{sg_}", s_pe)
                mark(f"gusl_{su_}", s_pe)
                p.emit("scalar", lambda e, tc=tc: e.activation(
                    out=sgt[:, tc * 512:(tc + 1) * 512], in_=pf[tc][:], func=AF.Silu),
                    wait=now(s_pe) + W(f"sgt_{tc}"), inc=[(s_act, 1)])
                mark(f"exb_{tc}", s_act)
                def iw(e, tc=tc, eg=eg, ps=ps, wsl=wsl):
                    e.tensor_tensor(out=itt[:, tc * 512:(tc + 1) * 512],
                                    in0=pf[2 + tc][:],
                                    in1=sgt[:, tc * 512:(tc + 1) * 512], op=ALU.mult)
                    return e.tensor_tensor(
                        out=interv[:, eg * 4 + ps, tc * 512:(tc + 1) * 512],
                        in0=itt[:, tc * 512:(tc + 1) * 512],
                        in1=wbc[:, wsl * TOK + tc * 512: wsl * TOK + (tc + 1) * 512],
                        op=ALU.mult)
                p.emit("vector", iw,
                       wait=now(s_act) + (W(f"dppe_{g - 1}") if eg == 0 and ps == 0 and tc == 0 else []),
                       inc=[(s_dve, 1)])
                mark(f"exb_{2 + tc}", s_dve)
                mark(f"sgt_{tc}", s_dve)
        if e_ == 0:
            mark("shint_last_unused", s_dve)

        if eg == G - 1:
            mark(f"inter_{g}", s_dve)
            for hc in range(KC):
                dsl = hc % 2
                p.emit("gpsimd", lambda e, hc=hc, g=g, dsl=dsl: e.dma_start(
                    out=dprv[:, dsl], in_=d_dp[hc, :, g * G:(g + 1) * G]),
                    wait=W(f"dpsl_{dsl}"), inc=[(s_dp[dsl], 16)])
                ldw = now(s_dp[dsl])
                for tc in range(2):
                    b = 4 + (hc * 2 + tc) % 2
                    wv_ = W(f"inter_{g}") + (list(ldw) if tc == 0 else []) + \
                          W(f"dpb_{b}")
                    def pmm(e, dsl=dsl, tc=tc, b=b):
                        last = None
                        for ee in range(G):
                            for ic in range(4):
                                last = e.matmul(
                                    pf[b][:], dprv[:, dsl, ee, ic, :],
                                    interv[:, ee * 4 + ic, tc * 512:(tc + 1) * 512],
                                    start=(ee == 0 and ic == 0),
                                    stop=(ee == G - 1 and ic == 3))
                        return last
                    p.emit("tensor", pmm, wait=wv_, inc=[(s_pe, 1)])
                    p.emit("vector", lambda e, hc=hc, tc=tc, b=b: e.tensor_tensor(
                        out=oav[:, hc, tc * 512:(tc + 1) * 512], in0=pf[b][:],
                        in1=oav[:, hc, tc * 512:(tc + 1) * 512], op=ALU.add),
                        wait=now(s_pe), inc=[(s_dve, 1)])
                    mark(f"dpb_{b}", s_dve)
                mark(f"dpsl_{dsl}", s_pe)
            mark(f"dppe_{g}", s_pe)
    mark("moe_done", s_dve)

    if dumps:
        dbg = {"r2": r2, "wfeat": wfeat, "g1b": g1b, "wtok": wtok, "ltile": ltile, "mx": mx, "expv": expv, "cur": cur,
               "inter": inter, "wbc": wbc, "sqb": sqb, "x2d": x2}
        for i, name in enumerate(dumps):
            tsr = dbg[name]
            shp = [int(x) for x in tsr[:].shape]
            dt_ = tsr[:].dtype
            od = nc.dram_tensor(f"dump_{name}", shp, dt_, kind="ExternalOutput").ap()
            p.emit("sync", lambda e, od=od, tsr=tsr: e.dma_start(out=od, in_=tsr[:]),
                   wait=W("moe_done") + now(s_dve, s_act, s_pe), inc=[(s_st, 16)])

    for t in range(8):
        osl = t % 2
        for g4 in range(4):
            b = g4 % 2
            wv_ = (W("moe_done") if t == 0 and g4 == 0 else []) + W(f"otr_{b}")
            def otr(e, t=t, g4=g4, b=b):
                last = None
                for u in range(4):
                    gg = g4 * 4 + u
                    last = e.transpose(pf[b][:, u * 128:(u + 1) * 128],
                                       oav[:, gg, t * 128:(t + 1) * 128], ident_f)
                return last
            p.emit("tensor", otr, wait=wv_, inc=[(s_pe, 1)])
            p.emit("vector", lambda e, g4=g4, b=b, osl=osl: e.tensor_copy(
                out=outstg[:, osl * HID + g4 * 512: osl * HID + (g4 + 1) * 512],
                in_=pf[b][:]),
                wait=now(s_pe) + ([(s_st, 16 * (t - 1))] if t >= 2 and g4 == 0 else []),
                inc=[(s_dve, 1)])
            mark(f"otr_{b}", s_dve)
        p.emit("sync", lambda e, t=t, osl=osl: e.dma_start(
            out=d_out[t * 128:(t + 1) * 128, :],
            in_=outstg[:, osl * HID:(osl + 1) * HID]),
            wait=now(s_dve), inc=[(s_st, 16)])
    p.emit("sync", lambda e, v=s_st.val: e.wait_ge(s_st.h, v))

    return nc, p, dram


def _consts():
    cbv = np.zeros((128, 256), ml_dtypes.bfloat16)
    cbv[:, 0:128] = np.eye(128, dtype=ml_dtypes.bfloat16)
    cbv[:, 128:256] = 1
    cfv = np.zeros((128, 772), np.float32)
    cfv[:, 0:128] = np.eye(128, dtype=np.float32)
    cfv[:, 128:256] = 1
    cfv[:, 256:768] = (np.arange(512, dtype=np.float32)[None, :]
                       - np.arange(128, dtype=np.float32)[:, None])
    cfv[:, 768:769] = 1e-5
    return cbv, cfv


def _tau_for_core(s):
    tau = np.zeros(N_TAU, np.float32)
    for (qc, kt, ti) in SCORE_CHUNKS:
        if ti is None:
            continue
        tau[ti] = (kt * 128 - qc * 512) if kt < 8 else (BIG if s == 0 else -BIG)
    return np.ascontiguousarray(np.broadcast_to(tau[None, :], (128, N_TAU)))


def prep_weights(inputs):
    d = {}
    def qkvperm(w):
        return np.ascontiguousarray(
            np.asarray(w, np.float32).reshape(16, 128, 16, 128).transpose(0, 3, 2, 1))
    d["wq"] = qkvperm(inputs["L0_attn_Wq"])
    d["wk"] = qkvperm(inputs["L0_attn_Wk"])
    d["wv"] = qkvperm(inputs["L0_attn_Wv"])
    d["wo"] = np.ascontiguousarray(
        np.asarray(inputs["L0_attn_Wo"], np.float32).reshape(2048, 16, 128)
        .transpose(2, 1, 0))
    d["gamma"] = np.ascontiguousarray(
        np.asarray(inputs["L0_attn_gamma"], np.float32).reshape(128, 1))
    d["beta"] = np.ascontiguousarray(
        np.asarray(inputs["L0_attn_beta"], np.float32).reshape(128, 1))
    gu = np.asarray(inputs["L0_ffn_gu_w"], np.float32)
    d["gu"] = np.ascontiguousarray(
        gu.reshape(64, 8, 128, 16, 128).transpose(0, 1, 4, 3, 2))
    dp = np.asarray(inputs["L0_ffn_dp_w"], np.float32)
    d["dp"] = np.ascontiguousarray(
        dp.reshape(64, 16, 128, 4, 128).transpose(1, 4, 0, 3, 2))
    d["gatew"] = np.ascontiguousarray(
        np.asarray(inputs["L0_ffn_gate_w"], np.float32).reshape(64, 16, 128)
        .transpose(2, 1, 0))
    def shperm(w):
        return np.asarray(w, np.float32).reshape(4, 128, 16, 128).transpose(0, 3, 2, 1)
    d["shg"] = np.ascontiguousarray(np.stack(
        [shperm(inputs["L0_ffn_sh_gate_w"]), shperm(inputs["L0_ffn_sh_up_w"])], axis=0))
    d["shd"] = np.ascontiguousarray(
        np.asarray(inputs["L0_ffn_sh_down_w"], np.float32)
        .reshape(4, 4, 128, 4, 128).transpose(0, 4, 1, 3, 2))
    d["shg1"] = np.ascontiguousarray(
        np.asarray(inputs["L0_ffn_sh_gate1_w"], np.float32).reshape(16, 128)
        .transpose(1, 0)).reshape(128, 16, 1)
    d["consts_b"], d["consts_f"] = _consts()
    return d


def kernel(**inputs):
    x = np.asarray(inputs["L0_attn_hidden_states"], np.float32)
    shared = prep_weights(inputs)
    nc, p, dram = build_program("full")
    p.finish()
    in_maps = []
    for c in range(8):
        b, s = c // 2, c % 2
        own = x[b, s * TOK:(s + 1) * TOK]
        other = x[b, (1 - s) * TOK:(2 - s) * TOK]
        m = dict(shared)
        m["x_seq"] = np.ascontiguousarray(np.concatenate([own, other], axis=0))
        m["x_own"] = np.ascontiguousarray(own)
        m["tau"] = _tau_for_core(s)
        in_maps.append(m)
    res = run_bass_kernel_spmd(nc, in_maps, core_ids=list(range(8)))
    out = np.empty((4, 2048, 2048), np.float32)
    for c in range(8):
        b, s = c // 2, c % 2
        out[b, s * TOK:(s + 1) * TOK] = res.results[c]["out"]
    return out


# version 03
# speedup vs baseline: 1.0103x; 1.0103x over previous
import numpy as np
import ml_dtypes
import concourse.bass as bass
import concourse.mybir as mybir
from concourse.bass_utils import run_bass_kernel_spmd

F32 = mybir.dt.float32
BF16 = mybir.dt.bfloat16
AF = mybir.ActivationFunctionType
ALU = mybir.AluOpType
AXX = mybir.AxisListType.X

HID = 2048
T = 2048
TOK = 1024
NH = 16
NE = 64
TOPK = 8
G = 4
KC = HID // 128
EPS = 1e-5
BIG = 1.0e9
INVH = 1.0 / HID
INVD = 1.0 / 128.0
QSC = 128.0 ** -0.5

SCORE_CHUNKS = []
_ti = 0
for _qc in range(2):
    for _kt in range(16):
        if _kt < 8:
            if _kt * 128 > _qc * 512 + 511:
                continue
            if _kt * 128 + 127 <= _qc * 512:
                SCORE_CHUNKS.append((_qc, _kt, None))
            else:
                SCORE_CHUNKS.append((_qc, _kt, _ti)); _ti += 1
        else:
            SCORE_CHUNKS.append((_qc, _kt, _ti)); _ti += 1
N_TAU = _ti
assert len(SCORE_CHUNKS) == 28 and N_TAU == 24


class Ctr:
    __slots__ = ("h", "name", "val")

    def __init__(self, h, name):
        self.h, self.name, self.val = h, name, 0


class Prog:

    def __init__(self, nc):
        self.nc = nc
        self.ev = {e: [] for e in ("sync", "gpsimd", "scalar", "vector", "tensor")}
        self.vctr = None

    def sem(self, name):
        return Ctr(self.nc.alloc_semaphore(name), name)

    def emit(self, eng, fn, wait=(), inc=()):
        wait = list(wait)
        inc = list(inc)
        if eng == "vector" and self.vctr is not None:
            wait.append((self.vctr, self.vctr.val))
            if not any(s is self.vctr for s, _ in inc):
                inc.append((self.vctr, 1))
        incs = []
        for s, n in inc:
            s.val += n
            incs.append((s, n))
        seen = {}
        for s, v in wait:
            if v > 0 and (id(s) not in seen or seen[id(s)][1] < v):
                seen[id(s)] = (s, v)
        self.ev[eng].append((fn, list(seen.values()), incs))

    def finish(self):
        with self.nc.Block() as blk:
            def mk(name):
                def body(eng):
                    for fn, waits, incs in self.ev[name]:
                        for s, v in waits:
                            eng.wait_ge(s.h, v)
                        last = fn(eng)
                        for s, n in incs:
                            last = last.then_inc(s.h, n)
                return body
            blk.sync(mk("sync"))
            blk.gpsimd(mk("gpsimd"))
            blk.scalar(mk("scalar"))
            blk.vector(mk("vector"))
            blk.tensor(mk("tensor"))


def build_program(mode="full", ne_run=NE, dumps=()):
    nc = bass.Bass("TRN2", target_bir_lowering=False, debug=False)
    p = Prog(nc)
    do_attn = mode in ("full", "attn")
    do_moe = mode in ("full", "moe")
    dram = {}

    def din(name, shape, dtype=F32):
        dram[name] = nc.dram_tensor(name, list(shape), dtype, kind="ExternalInput").ap()
        return dram[name]

    def dout(name, shape, dtype=F32):
        dram[name] = nc.dram_tensor(name, list(shape), dtype, kind="ExternalOutput").ap()
        return dram[name]

    if do_attn:
        d_xseq = din("x_seq", (T, HID))
        d_wq = din("wq", (NH, 128, KC, 128))
        d_wk = din("wk", (NH, 128, KC, 128))
        d_wv = din("wv", (NH, 128, KC, 128))
        d_wo = din("wo", (128, KC, HID))
        d_gam = din("gamma", (128, 1))
        d_bet = din("beta", (128, 1))
        d_tau = din("tau", (128, N_TAU))
        d_xown = din("x_own", (TOK, HID))
    if mode == "moe":
        d_x2in = din("x2_in", (TOK, HID))
    if do_moe:
        d_gu = din("gu", (NE, 8, 128, KC, 128))
        d_dp = din("dp", (KC, 128, NE, 4, 128))
        d_gatew = din("gatew", (128, KC, NE))
        d_shg = din("shg", (2, 4, 128, KC, 128))
        d_shd = din("shd", (4, 128, 4, 4, 128))
        d_shg1 = din("shg1", (128, KC, 1))
    d_cb = din("consts_b", (128, 256), BF16)
    d_cf = din("consts_f", (128, 772), F32)
    d_out = dout("x2_out" if mode == "attn" else "out", (TOK, HID))

    cb = nc.alloc_sbuf_tensor("cb", [128, 256], BF16)
    cf = nc.alloc_sbuf_tensor("cf", [128, 772], F32)
    ident_b, ones_col_b = cb[:, 0:128], cb[:, 128:129]
    ones_row_b = cb[0:1, 128:256]
    ident_f, ones_col_f = cf[:, 0:128], cf[:, 128:129]
    ones_row_f = cf[0:1, 128:256]
    iota_fp = cf[:, 256:768]
    epsv = cf[:, 768:769]
    epsr = cf[0:1, 768:769]
    scal = nc.alloc_sbuf_tensor("scal", [128, 16], F32)
    if do_attn:
        gam = nc.alloc_sbuf_tensor("gam", [128, 1], F32)
        bet = nc.alloc_sbuf_tensor("bet", [128, 1], F32)
        taus = nc.alloc_sbuf_tensor("taus", [128, N_TAU], F32)
    if do_moe:
        gwb = nc.alloc_sbuf_tensor("gwb", [128, KC * NE], BF16)
        gwv = gwb[:].rearrange("p (c n) -> p c n", c=KC)
        shg1b = nc.alloc_sbuf_tensor("shg1b", [128, KC], BF16)
        ltile = nc.alloc_sbuf_tensor("ltile", [128, NE], F32)
        cur = nc.alloc_sbuf_tensor("cur", [128, NE], F32)
        mskt = nc.alloc_sbuf_tensor("mskt", [128, NE], F32)
        expv = nc.alloc_sbuf_tensor("expv", [128, NE], F32)
        mx = nc.alloc_sbuf_tensor("mx", [128, TOPK], F32)
        wtok = nc.alloc_sbuf_tensor("wtok", [128, NE], F32)
        wfeat = nc.alloc_sbuf_tensor("wfeat", [64, TOK], F32)
        wrow = nc.alloc_sbuf_tensor("wrow", [1, TOK], BF16)
        g1r = nc.alloc_sbuf_tensor("g1r", [1, TOK], F32)
        g1b = nc.alloc_sbuf_tensor("g1b", [128, TOK], F32)
        oinsp = nc.alloc_sbuf_tensor("oinsp", [128, 512], F32)


    SB_END = 16512 + 212832
    _free0 = nc.sbuf_bytes_remaining
    BA = (SB_END - _free0 + 127) // 128 * 128

    def at(name, shape, dtype, off):
        nb = int(np.prod(shape[1:])) * mybir.dt.size(dtype)
        assert BA + off + nb <= SB_END, f"{name}: {BA + off + nb} > {SB_END}"
        return nc.alloc_sbuf_tensor_at(name, list(shape), dtype, offset=BA + off)

    if do_attn:
        r1 = at("r1", (128, KC * T), BF16, 0)
        r1v = r1[:].rearrange("p (c t) -> p c t", c=KC)
    x2 = at("x2", (128, 8 * HID), F32, 0)
    if do_moe:
        inter = at("inter", (128, G * 4 * TOK), BF16, 0)
        interv = inter[:].rearrange("p (i t) -> p i t", i=G * 4)
        sgt = at("sgt", (128, 2 * 512), BF16, 32768)
        itt = at("itt", (128, 2 * 512), BF16, 32768 + 2048)
        exsp = at("exsp", (128, 512), F32, 32768 + 4096)
    RB = 65536
    if do_attn:
        xin = at("xin", (128, 3 * HID), F32, RB)
        sqb_l1 = at("sqb_l1", (128, HID), BF16, RB + 24576)
        ofeat = at("ofeat", (128, KC * TOK), BF16, RB)
        ofv = ofeat[:].rearrange("p (c t) -> p c t", c=KC)
        rowf = at("rowf", (1, 8 * 512), F32, RB + 32768)
        rowb = at("rowb", (1, 12 * 512), BF16, RB + 32768 + 16384)
        y2scr = at("y2scr", (128, 512), BF16, RB + 32768 + 28672)
        wob = at("wob", (128, 2 * KC * 512), BF16, RB + 32768)
        wobv = [wob[:, i * KC * 512:(i + 1) * KC * 512].rearrange(
            "p (c n) -> p c n", c=KC) for i in range(2)]
    if do_moe:
        sqb = at("sqb", (128, HID), BF16, RB + 32768)
        oacc = at("oacc", (128, KC * TOK), F32, RB)
        oav = oacc[:].rearrange("p (c t) -> p c t", c=KC)
    RC = 131072
    if do_attn:
        y0k2 = [at(f"y0k{i}", (128, T), BF16, RC + i * 4096) for i in range(2)]
        vtok2 = [at(f"vtok{i}", (128, 16 * 128), BF16, RC + 8192 + i * 4096)
                 for i in range(2)]
        y0q2 = [at(f"y0q{i}", (128, TOK), BF16, RC + 16384 + i * 2048)
                for i in range(2)]
        vft = at("vft", (128, T), BF16, RC + 20480)
        ssq = at("ssq", (128, 4 * 512), BF16, RC + 24576)
        zbs = at("zbs", (128, 512), F32, RB + 32768 + 29696)
        nrmt = at("nrmt", (128, 512), BF16, RB + 32768 + 31744)
        zrow = at("zrow", (1, 512), F32, RC + 28672)
    if do_moe:
        r2 = at("r2", (128, KC * TOK), BF16, RC)
        r2v = r2[:].rearrange("p (c t) -> p c t", c=KC)
    RE = 163840
    if do_attn:
        wqkv = at("wqkv", (128, 2 * 3 * KC * 128), BF16, RE)
        wvv = wqkv[:].rearrange("p (s w c j) -> p s w c j", s=2, w=3, c=KC)
        xores = at("xores", (128, 2 * 512), F32, RE)
    if do_moe:
        gur = at("gur", (128, 3 * KC * 128), BF16, RE)
        gurv = gur[:].rearrange("p (s c j) -> p s c j", s=3, c=KC)
        dpr = at("dpr", (128, 2 * 16 * 128), BF16, RE + 12288)
        dprv = dpr[:].rearrange("p (s e i j) -> p s e i j", s=2, e=G, i=4)
        dprs = dpr[:].rearrange("p (s h i j) -> p s h i j", s=2, h=4, i=4)
        shsg = at("shsg", (128, 4 * 512), BF16, RE + 12288 + 8192)
        shsv = shsg[:].rearrange("p (m t) -> p m t", m=4)
        wbc = at("wbc", (128, 2 * TOK), BF16, RE + 12288 + 8192)
    outstg = at("outstg", (128, 2 * HID), F32, RE)

    pf = [nc.alloc_psum_tensor(f"pf{i}", [128, 512], F32) for i in range(7)]
    pbf = nc.alloc_psum_tensor("pbf", [128, 1024], BF16)

    s_pe = p.sem("s_pe")
    s_act = p.sem("s_act")
    s_dve = p.sem("s_dve")
    p.vctr = s_dve
    s_wqkv = [p.sem("s_wqkv0"), p.sem("s_wqkv1")]
    s_wo = [p.sem("s_wo0"), p.sem("s_wo1")]
    s_gu = [p.sem(f"s_gu{i}") for i in range(3)]
    s_dp = [p.sem(f"s_dp{i}") for i in range(2)]
    s_one = p.sem("s_one")
    s_x = p.sem("s_x")
    s_xr = [p.sem(f"s_xr{i}") for i in range(3)]
    s_xo = [p.sem(f"s_xo{i}") for i in range(2)]
    s_so = [p.sem(f"s_so{i}") for i in range(2)]
    s_st = p.sem("s_st")
    s_wr = p.sem("s_wr")

    chk = {}

    def mark(key, ctr):
        chk[key] = (ctr, ctr.val)

    def W(*keys):
        return [chk[k] for k in keys if k in chk]

    def now(*ctrs):
        return [(c, c.val) for c in ctrs]

    def soft():
        return now(s_pe, s_act, s_dve)

    p.emit("sync", lambda e: e.dma_start(out=cb[:], in_=d_cb), inc=[(s_x, 16)])
    p.emit("sync", lambda e: e.dma_start(out=cf[:], in_=d_cf), inc=[(s_x, 16)])
    if do_attn:
        p.emit("sync", lambda e: e.dma_start(out=gam[:], in_=d_gam), inc=[(s_x, 16)])
        p.emit("sync", lambda e: e.dma_start(out=bet[:], in_=d_bet), inc=[(s_x, 16)])
        p.emit("sync", lambda e: e.dma_start(out=taus[:], in_=d_tau), inc=[(s_x, 16)])
    if do_moe:
        p.emit("gpsimd", lambda e: e.dma_start(out=gwv, in_=d_gatew), inc=[(s_one, 16)])
        p.emit("gpsimd", lambda e: e.dma_start(
            out=shg1b[:].rearrange("p (c n) -> p c n", n=1), in_=d_shg1),
            inc=[(s_one, 16)])
    base_w = now(s_x, s_one)

    def ln_tile(x_ap, out_ap, extra_wait=()):
        p.emit("vector", lambda e: e.reduce_sum(scal[:, 0:1], x_ap, axis=AXX),
               wait=tuple(extra_wait), inc=[(s_dve, 1)])
        for cc in range(4):
            p.emit("scalar", lambda e, cc=cc: e.activation(
                out=pf[cc % 2][:], in_=x_ap[:, cc * 512:(cc + 1) * 512],
                func=AF.Square, accum_out=scal[:, 4 + cc:5 + cc]),
                wait=list(extra_wait) + now(s_dve, s_pe) + now(s_act),
                inc=[(s_act, 1)])
        def mid1(e):
            e.reduce_sum(scal[:, 1:2], scal[:, 4:8], axis=AXX)
            return e.tensor_scalar(out=scal[:, 0:1], in0=scal[:, 0:1], scalar1=INVH,
                                   scalar2=None, op0=ALU.mult)
        p.emit("vector", mid1, wait=now(s_act))
        def mid2(e):
            e.tensor_scalar(out=scal[:, 1:2], in0=scal[:, 1:2], scalar1=INVH,
                            scalar2=None, op0=ALU.mult)
            return e.tensor_tensor(out=scal[:, 2:3], in0=scal[:, 0:1],
                                   in1=scal[:, 0:1], op=ALU.mult)
        p.emit("vector", mid2)
        p.emit("vector", lambda e: e.tensor_tensor(out=scal[:, 1:2], in0=scal[:, 1:2],
                                                   in1=scal[:, 2:3], op=ALU.subtract))
        p.emit("scalar", lambda e: e.activation(out=scal[:, 3:4], in_=scal[:, 1:2],
                                                func=AF.Sqrt, bias=epsv),
               wait=now(s_dve), inc=[(s_act, 1)])
        p.emit("vector", lambda e: e.reciprocal(out=scal[:, 3:4], in_=scal[:, 3:4]),
               wait=now(s_act))
        p.emit("vector", lambda e: e.tensor_scalar(
            out=out_ap, in0=x_ap, scalar1=scal[:, 0:1], scalar2=scal[:, 3:4],
            op0=ALU.subtract, op1=ALU.mult))

    def transpose_tile(src_ap, dst_fn, nchunk):
        for j0 in range(0, nchunk, 8):
            jn = min(8, nchunk - j0)
            def tr(e, j0=j0, jn=jn):
                last = None
                for j in range(jn):
                    last = e.transpose(pbf[:, j * 128:(j + 1) * 128],
                                       src_ap[:, (j0 + j) * 128:(j0 + j + 1) * 128],
                                       ident_b)
                return last
            p.emit("tensor", tr, wait=now(s_dve), inc=[(s_pe, 1)])
            def ev(e, j0=j0, jn=jn):
                last = None
                for j in range(jn):
                    last = e.tensor_copy(out=dst_fn(j0 + j),
                                         in_=pbf[:, j * 128:(j + 1) * 128])
                return last
            p.emit("vector", ev, wait=now(s_pe), inc=[(s_dve, 1)])

    if do_attn:
        for t in range(16):
            slot = t % 3
            xt = xin[:, slot * HID:(slot + 1) * HID]
            p.emit("sync", lambda e, xt=xt, t=t: e.dma_start(
                out=xt, in_=d_xseq[t * 128:(t + 1) * 128, :]),
                wait=W(f"l1fin_{t - 3}"), inc=[(s_xr[slot], 16)])
            ln_tile(xt, sqb_l1[:], extra_wait=now(s_xr[slot]) + (base_w if t == 0 else []))
            mark(f"l1fin_{t}", s_dve)
            transpose_tile(sqb_l1[:], lambda j, t=t: r1v[:, j, t * 128:(t + 1) * 128], KC)
        mark("r1_done", s_dve)

        def stageA1(h):
            sl = h % 2
            for wi, dsrc in enumerate((d_wq, d_wk, d_wv)):
                p.emit("gpsimd", lambda e, wi=wi, dsrc=dsrc, sl=sl, h=h: e.dma_start(
                    out=wvv[:, sl, wi, :, :], in_=dsrc[h]),
                    wait=W(f"hproj_{h - 2}") if wi == 0 else [],
                    inc=[(s_wqkv[sl], 16)])
            ldw = now(s_wqkv[sl])

            def proj(dst_fn, wsel, nck, fm, first):
                for q2 in range(nck):
                    b = q2 % 2
                    wv_ = list(ldw) if q2 == 0 else []
                    if first and q2 == 0:
                        wv_ += W("r1_done") + now(s_act)
                    wv_ += W(f"hpb_{b}")
                    def mm(e, q2=q2, b=b, wsel=wsel, sl=sl):
                        last = None
                        for c in range(KC):
                            last = e.matmul(pf[b][:], wvv[:, sl, wsel, c, :],
                                            r1v[:, c, q2 * 512:(q2 + 1) * 512],
                                            start=(c == 0), stop=(c == KC - 1))
                        return last
                    p.emit("tensor", mm, wait=wv_, inc=[(s_pe, 1)])
                    def evac(e, q2=q2, b=b, dst_fn=dst_fn, fm=fm):
                        dst = dst_fn(q2)
                        if fm:
                            return e.tensor_scalar(out=dst, in0=pf[b][:], scalar1=gam[:],
                                                   scalar2=bet[:], op0=ALU.mult,
                                                   op1=ALU.add)
                        return e.tensor_copy(out=dst, in_=pf[b][:])
                    p.emit("vector", evac, wait=now(s_pe), inc=[(s_dve, 1)])
                    mark(f"hpb_{b}", s_dve)
            par = h % 2
            proj(lambda q2: y0q2[par][:, q2 * 512:(q2 + 1) * 512], 0, 2, True, h == 0)
            proj(lambda q2: y0k2[par][:, q2 * 512:(q2 + 1) * 512], 1, 4, True, False)
            proj(lambda q2: vft[:, q2 * 512:(q2 + 1) * 512], 2, 4, False, False)
            mark(f"hproj_{h}", s_pe)
            transpose_tile(vft[:], lambda j, par=par: vtok2[par][:, j * 128:(j + 1) * 128], 16)

            def fmstats(y0, nck, aoff):
                for q2 in range(nck):
                    ysl = y0[:, q2 * 512:(q2 + 1) * 512]
                    p.emit("tensor", lambda e, ysl=ysl: e.matmul(
                        pf[6][0:1, :], ones_col_b, ysl, start=True, stop=True),
                        wait=now(s_dve) + W("pf6_free"), inc=[(s_pe, 1)])
                    p.emit("vector", lambda e: e.tensor_scalar(
                        out=rowf[0:1, 0:512], in0=pf[6][0:1, :], scalar1=INVD,
                        scalar2=None, op0=ALU.mult), wait=now(s_pe), inc=[(s_dve, 1)])
                    p.emit("scalar", lambda e, ysl=ysl: e.activation(
                        out=y2scr[:], in_=ysl, func=AF.Square),
                        wait=now(s_pe), inc=[(s_act, 1)])
                    p.emit("tensor", lambda e: e.matmul(
                        pf[6][0:1, :], ones_col_b, y2scr[:], start=True, stop=True),
                        wait=now(s_act, s_dve), inc=[(s_pe, 1)])
                    p.emit("vector", lambda e: e.tensor_tensor(
                        out=rowf[0:1, 512:1024], in0=rowf[0:1, 0:512],
                        in1=rowf[0:1, 0:512], op=ALU.mult), wait=now(s_pe))
                    p.emit("vector", lambda e: e.scalar_tensor_tensor(
                        out=rowf[0:1, 1024:1536], in0=pf[6][0:1, :], scalar=INVD,
                        in1=rowf[0:1, 512:1024], op0=ALU.mult, op1=ALU.subtract))
                    mark("pf6_free", s_dve)
                    p.emit("scalar", lambda e: e.activation(
                        out=rowf[0:1, 1536:2048], in_=rowf[0:1, 1024:1536],
                        func=AF.Sqrt, bias=epsr), wait=now(s_dve), inc=[(s_act, 1)])
                    p.emit("vector", lambda e: e.reciprocal(
                        out=rowf[0:1, 1536:2048], in_=rowf[0:1, 1536:2048]),
                        wait=now(s_act))
                    scale = QSC if aoff == 0 else 1.0
                    if scale != 1.0:
                        p.emit("vector", lambda e, scale=scale: e.tensor_scalar(
                            out=rowf[0:1, 1536:2048], in0=rowf[0:1, 1536:2048],
                            scalar1=scale, scalar2=None, op0=ALU.mult))
                    p.emit("vector", lambda e: e.scalar_tensor_tensor(
                        out=rowf[0:1, 512:1024], in0=rowf[0:1, 0:512], scalar=-1.0,
                        in1=rowf[0:1, 1536:2048], op0=ALU.mult, op1=ALU.mult))
                    def abcp(e, q2=q2, aoff=aoff):
                        e.tensor_copy(out=rowb[0:1, aoff + q2 * 1024:
                                                  aoff + q2 * 1024 + 512],
                                      in_=rowf[0:1, 1536:2048])
                        return e.tensor_copy(out=rowb[0:1, aoff + q2 * 1024 + 512:
                                                      aoff + q2 * 1024 + 1024],
                                             in_=rowf[0:1, 512:1024])
                    p.emit("vector", abcp, wait=W("bc_prev"))
            fmstats(y0q2[par], 2, 0)
            fmstats(y0k2[par], 4, 2048)

        def stageA2(h):
            par = h % 2
            for (y0, nck, aoff) in ((y0q2[par], 2, 0), (y0k2[par], 4, 2048)):
                for q2 in range(nck):
                    a0 = aoff + q2 * 1024
                    p.emit("tensor", lambda e, a0=a0: e.matmul(
                        pf[2][:], ones_row_b, rowb[0:1, a0:a0 + 512],
                        start=True, stop=True),
                        wait=now(s_dve) + W("nrm_prev"), inc=[(s_pe, 1)])
                    p.emit("tensor", lambda e, a0=a0: e.matmul(
                        pf[3][:], ones_row_b, rowb[0:1, a0 + 512:a0 + 1024],
                        start=True, stop=True), inc=[(s_pe, 1)])
                    p.emit("vector", lambda e, y0=y0, q2=q2: e.tensor_tensor(
                        out=nrmt[:], in0=y0[:, q2 * 512:(q2 + 1) * 512],
                        in1=pf[2][:], op=ALU.mult), wait=now(s_pe))
                    p.emit("vector", lambda e, y0=y0, q2=q2: e.tensor_tensor(
                        out=y0[:, q2 * 512:(q2 + 1) * 512], in0=nrmt[:],
                        in1=pf[3][:], op=ALU.add))
                    mark("nrm_prev", s_dve)
            mark(f"fmdone_{h}", s_dve)
            mark("bc_prev", s_pe)

        def stageB(h):
            par = h % 2
            qfm_, kfm_, vtok_ = y0q2[par], y0k2[par], vtok2[par]
            for q2 in range(2):
                lst = [(qc, kt, ti) for (qc, kt, ti) in SCORE_CHUNKS if qc == q2]
                nl = len(lst)
                def emit_score(ci):
                    qc, kt, ti = lst[ci]
                    sb_ = ci % 2
                    ssl = ssq[:, (ci % 4) * 512:(ci % 4 + 1) * 512]
                    wv_ = W(f"sqev_{sb_}")
                    if ci == 0:
                        wv_ += W(f"fmdone_{h}") + now(s_dve)
                    p.emit("tensor", lambda e, kt=kt, qc=qc, sb_=sb_, kfm_=kfm_,
                           qfm_=qfm_: e.matmul(
                        pf[2 + sb_][:], kfm_[:, kt * 128:(kt + 1) * 128],
                        qfm_[:, qc * 512:(qc + 1) * 512], start=True, stop=True),
                        wait=wv_, inc=[(s_pe, 1)])
                    p.emit("scalar", lambda e, sb_=sb_, ssl=ssl: e.activation(
                        out=ssl, in_=pf[2 + sb_][:], func=AF.Square),
                        wait=now(s_pe) + W(f"avz_{ci % 4}"), inc=[(s_act, 1)])
                    mark(f"sqev_{sb_}", s_act)
                    if ti is not None:
                        p.emit("vector", lambda e, ssl=ssl, ti=ti:
                               e.scalar_tensor_tensor(out=ssl, in0=iota_fp,
                                                      scalar=taus[:, ti:ti + 1],
                                                      in1=ssl, op0=ALU.is_ge,
                                                      op1=ALU.mult),
                               wait=now(s_act), inc=[(s_dve, 1)])
                    mark(f"smask_{ci % 4}", s_dve if ti is not None else s_act)
                def emit_avz(ci):
                    qc, kt, ti = lst[ci]
                    ssl = ssq[:, (ci % 4) * 512:(ci % 4 + 1) * 512]
                    mk = chk.get(f"smask_{ci % 4}")
                    def avz(e, kt=kt, q2=q2, ssl=ssl, first=(ci == 0),
                            last_=(ci == nl - 1), vtok_=vtok_):
                        e.matmul(pf[4 + q2][:], vtok_[:, kt * 128:(kt + 1) * 128],
                                 ssl, start=first, stop=last_, skip_group_check=True)
                        return e.matmul(pf[6][0:1, :], ones_col_b, ssl,
                                        start=first, stop=last_, skip_group_check=True)
                    wv_ = [mk] if mk else []
                    if ci == 0:
                        wv_ += W(f"oev_{q2}") + W("zread") + W("pf6_free")
                    p.emit("tensor", avz, wait=wv_, inc=[(s_pe, 1)])
                    mark(f"avz_{ci % 4}", s_pe)
                for ci in range(nl):
                    emit_score(ci)
                    if ci >= 2:
                        emit_avz(ci - 2)
                emit_avz(nl - 2)
                emit_avz(nl - 1)
                p.emit("vector", lambda e: e.tensor_scalar(
                    out=zrow[0:1, :], in0=pf[6][0:1, :], scalar1=EPS,
                    scalar2=None, op0=ALU.add), wait=now(s_pe))
                p.emit("vector", lambda e: e.reciprocal(out=zrow[0:1, :],
                                                        in_=zrow[0:1, :]))
                mark("zread", s_dve)
                mark("pf6_free", s_dve)
                p.emit("tensor", lambda e: e.matmul(
                    pf[2][:], ones_row_f, zrow[0:1, :], start=True, stop=True),
                    wait=now(s_dve) + W("sqev_0"), inc=[(s_pe, 1)])
                p.emit("scalar", lambda e: e.copy(out=zbs[:], in_=pf[2][:]),
                       wait=now(s_pe) + W(f"oev_{1 - q2}"), inc=[(s_act, 1)])
                p.emit("vector", lambda e, h=h, q2=q2: e.tensor_tensor(
                    out=ofv[:, h, q2 * 512:(q2 + 1) * 512], in0=pf[4 + q2][:],
                    in1=zbs[:], op=ALU.mult), wait=now(s_act), inc=[(s_dve, 1)])
                mark(f"oev_{q2}", s_dve)
            mark(f"hdone_{h}", s_dve)

        import os as _os
        if _os.environ.get("NO_PIPE") == "1":
            for h in range(NH):
                stageA1(h)
                stageA2(h)
                stageB(h)
        else:
            for h in range(NH):
                if h >= 1:
                    stageB(h - 1)
                stageA1(h)
                stageA2(h)
            stageB(NH - 1)
        mark(f"hproj_{NH - 1}", s_pe)

        for n in range(4):
            wsl = n % 2
            p.emit("gpsimd", lambda e, n=n, wsl=wsl: e.dma_start(
                out=wobv[wsl], in_=d_wo[:, :, n * 512:(n + 1) * 512]),
                wait=(W(f"hdone_{NH - 1}") if n == 0 else W(f"wopg_{n - 2}")),
                inc=[(s_wo[wsl], 16)])
            ldw = now(s_wo[wsl])
            for m in range(8):
                xsl = m % 2
                p.emit("sync", lambda e, n=n, m=m, xsl=xsl: e.dma_start(
                    out=xores[:, xsl * 512:(xsl + 1) * 512],
                    in_=d_xown[m * 128:(m + 1) * 128, n * 512:(n + 1) * 512]),
                    wait=(W(f"hproj_{NH - 1}") if (n == 0 and m == 0) else []) +
                         W(f"xores_{xsl}"), inc=[(s_xo[xsl], 16)])
                ldx = now(s_xo[xsl])
                b = m % 2
                wv_ = (list(ldw) if m == 0 else []) + W(f"wob_{b}")
                def wmm(e, m=m, wsl=wsl, b=b):
                    last = None
                    for c in range(KC):
                        last = e.matmul(pf[b][:], ofv[:, c, m * 128:(m + 1) * 128],
                                        wobv[wsl][:, c, :], start=(c == 0),
                                        stop=(c == KC - 1))
                    return last
                p.emit("tensor", wmm, wait=wv_, inc=[(s_pe, 1)])
                p.emit("vector", lambda e, n=n, m=m, b=b, xsl=xsl: e.tensor_tensor(
                    out=x2[:, m * HID + n * 512: m * HID + (n + 1) * 512],
                    in0=pf[b][:], in1=xores[:, xsl * 512:(xsl + 1) * 512], op=ALU.add),
                    wait=now(s_pe) + ldx, inc=[(s_dve, 1)])
                mark(f"wob_{b}", s_dve)
                mark(f"xores_{xsl}", s_dve)
            mark(f"wopg_{n}", s_pe)
        mark("x2_done", s_dve)

        if mode == "attn":
            if dumps:
                dbg = {"r1": r1, "y0q": y0q2[1], "y0k": y0k2[1],
                       "vtok": vtok2[1], "ofeat": ofeat, "zbs": zbs, "vft": vft}
                for name in dumps:
                    tsr = dbg[name]
                    shp = [int(xx) for xx in tsr[:].shape]
                    od = nc.dram_tensor(f"dump_{name}", shp, tsr[:].dtype,
                                        kind="ExternalOutput").ap()
                    p.emit("sync", lambda e, od=od, tsr=tsr: e.dma_start(
                        out=od, in_=tsr[:]),
                        wait=W("x2_done") + now(s_dve, s_act, s_pe), inc=[(s_st, 16)])
            for t in range(8):
                p.emit("sync", lambda e, t=t: e.dma_start(
                    out=d_out[t * 128:(t + 1) * 128, :],
                    in_=x2[:, t * HID:(t + 1) * HID]),
                    wait=W("x2_done"), inc=[(s_st, 16)])
            p.emit("sync", lambda e, v=s_st.val: e.wait_ge(s_st.h, v))
            return nc, p, dram

    if mode == "moe":
        for t in range(8):
            p.emit("sync", lambda e, t=t: e.dma_start(
                out=x2[:, t * HID:(t + 1) * HID],
                in_=d_x2in[t * 128:(t + 1) * 128, :]), inc=[(s_x, 16)])
        p.emit("vector", lambda e: e.engine_nop(),
               wait=now(s_x) + base_w, inc=[(s_dve, 1)])
        mark("x2_done", s_dve)

    for t in range(8):
        ln_tile(x2[:, t * HID:(t + 1) * HID], sqb[:],
                extra_wait=(W("x2_done") + soft()) if t == 0 else ())
        transpose_tile(sqb[:], lambda j, t=t: r2v[:, j, t * 128:(t + 1) * 128], KC)
    mark("r2_done", s_dve)

    for tc in range(2):
        def g1mm(e, tc=tc):
            last = None
            for c in range(KC):
                last = e.matmul(pf[6][0:1, :], shg1b[:, c:c + 1],
                                r2v[:, c, tc * 512:(tc + 1) * 512],
                                start=(c == 0), stop=(c == KC - 1))
            return last
        p.emit("tensor", g1mm, wait=now(s_dve, s_act), inc=[(s_pe, 1)])
        p.emit("scalar", lambda e, tc=tc: e.activation(
            out=g1r[0:1, tc * 512:(tc + 1) * 512], in_=pf[6][0:1, :], func=AF.Sigmoid),
            wait=now(s_pe), inc=[(s_act, 1)])
        p.emit("tensor", lambda e, tc=tc: e.matmul(
            pf[6][:], ones_row_f, g1r[0:1, tc * 512:(tc + 1) * 512],
            start=True, stop=True), wait=now(s_act), inc=[(s_pe, 1)])
        p.emit("scalar", lambda e, tc=tc: e.copy(
            out=g1b[:, tc * 512:(tc + 1) * 512], in_=pf[6][:]),
            wait=now(s_pe), inc=[(s_act, 1)])
    mark("g1_done", s_act)

    for tc in range(2):
        for gi in range(2):
            for mc in range(4):
                ci = tc * 8 + gi * 4 + mc
                slot = ci % 2
                p.emit("gpsimd", lambda e, gi=gi, mc=mc, slot=slot: e.dma_start(
                    out=gurv[:, slot, :, :], in_=d_shg[gi, mc]),
                    wait=(W("x2_done") if ci == 0 else W(f"gusl_{slot}")),
                    inc=[(s_gu[slot], 16)])
                ldw = now(s_gu[slot])
                b = mc % 2 if gi == 0 else 2 + mc % 2
                wv_ = list(ldw) + W(f"shb_{b}")
                def smm(e, slot=slot, tc=tc, b=b):
                    last = None
                    for c in range(KC):
                        last = e.matmul(pf[b][:], gurv[:, slot, c, :],
                                        r2v[:, c, tc * 512:(tc + 1) * 512],
                                        start=(c == 0), stop=(c == KC - 1))
                    return last
                p.emit("tensor", smm, wait=wv_, inc=[(s_pe, 1)])
                mark(f"gusl_{slot}", s_pe)
                if gi == 0:
                    p.emit("scalar", lambda e, mc=mc, b=b: e.activation(
                        out=shsv[:, mc, :], in_=pf[b][:], func=AF.Silu),
                        wait=now(s_pe) + W(f"shsl_{mc}"), inc=[(s_act, 1)])
                    mark(f"shb_{b}", s_act)
                else:
                    p.emit("vector", lambda e, mc=mc, b=b: e.tensor_tensor(
                        out=shsv[:, mc, :], in0=pf[b][:],
                        in1=shsv[:, mc, :], op=ALU.mult),
                        wait=now(s_pe, s_act), inc=[(s_dve, 1)])
                    mark(f"shb_{b}", s_dve)
        for l4 in range(4):
            slot = l4 % 2
            p.emit("gpsimd", lambda e, l4=l4, slot=slot: e.dma_start(
                out=dprs[:, slot], in_=d_shd[l4]),
                wait=W(f"dpsl_{slot}"), inc=[(s_dp[slot], 16)])
            ldw = now(s_dp[slot])
            for hl in range(4):
                hm = l4 * 4 + hl
                b = hl % 2
                wv_ = (list(ldw) if hl == 0 else []) + W(f"sdb_{b}") + \
                      (now(s_dve, s_act) if l4 == 0 and hl == 0 else [])
                def dmm(e, slot=slot, hl=hl, tc=tc, b=b):
                    last = None
                    for ic in range(4):
                        last = e.matmul(pf[b][:], dprs[:, slot, hl, ic, :],
                                        shsv[:, ic, :],
                                        start=(ic == 0), stop=(ic == 3))
                    return last
                p.emit("tensor", dmm, wait=wv_, inc=[(s_pe, 1)])
                def xtr(e, hm=hm, tc=tc, b=b):
                    last = None
                    for u in range(4):
                        tt = tc * 4 + u
                        last = e.transpose(
                            pf[2 + b][:, u * 128:(u + 1) * 128],
                            x2[:, tt * HID + hm * 128: tt * HID + (hm + 1) * 128],
                            ident_f)
                    return last
                p.emit("tensor", xtr, wait=W(f"sdt_{b}"), inc=[(s_pe, 1)])
                p.emit("vector", lambda e, tc=tc, b=b: e.tensor_tensor(
                    out=oinsp[:], in0=pf[b][:],
                    in1=g1b[:, tc * 512:(tc + 1) * 512], op=ALU.mult),
                    wait=now(s_pe) + W("g1_done"))
                p.emit("vector", lambda e, hm=hm, tc=tc, b=b: e.tensor_tensor(
                    out=oav[:, hm, tc * 512:(tc + 1) * 512],
                    in0=pf[2 + b][:], in1=oinsp[:], op=ALU.add))
                mark(f"sdb_{b}", s_dve)
                mark(f"sdt_{b}", s_dve)
            mark(f"dpsl_{slot}", s_pe)
        for mc in range(4):
            mark(f"shsl_{mc}", s_pe)
        for _b in range(4):
            mark(f"shb_{_b}", s_dve)
    mark("oacc_init", s_dve)

    for m in range(8):
        b = m % 2
        wv_ = W(f"route_{b}") + (soft() if m == 0 else []) + \
              (now(s_one) if m == 0 else [])
        def lmm(e, m=m, b=b):
            last = None
            for c in range(KC):
                last = e.matmul(pf[b][:, 0:NE], r2v[:, c, m * 128:(m + 1) * 128],
                                gwv[:, c, :], start=(c == 0), stop=(c == KC - 1))
            return last
        p.emit("tensor", lmm, wait=wv_, inc=[(s_pe, 1)])
        def lcp(e, b=b):
            e.tensor_copy(out=ltile[:], in_=pf[b][:, 0:NE])
            return e.tensor_copy(out=cur[:], in_=pf[b][:, 0:NE])
        p.emit("vector", lcp, wait=now(s_pe, s_act))
        for r in range(TOPK):
            p.emit("vector", lambda e, r=r: e.reduce_max(mx[:, r:r + 1], cur[:],
                                                         axis=AXX))
            if r < TOPK - 1:
                p.emit("vector", lambda e, r=r: e.tensor_single_scalar(
                    out=mskt[:], in_=cur[:], scalar=mx[:, r:r + 1], op=ALU.is_ge))
                p.emit("vector", lambda e: e.scalar_tensor_tensor(
                    out=cur[:], in0=mskt[:], scalar=-BIG, in1=cur[:],
                    op0=ALU.mult, op1=ALU.add))
        p.emit("vector", lambda e: e.tensor_scalar(
            out=scal[:, 9:10], in0=mx[:, 0:1], scalar1=-1.0, scalar2=None,
            op0=ALU.mult))
        mark(f"route_{b}", s_dve)
        p.emit("scalar", lambda e: e.activation(out=expv[:], in_=ltile[:], func=AF.Exp,
                                                bias=scal[:, 9:10]),
               wait=now(s_dve), inc=[(s_act, 1)])
        p.emit("vector", lambda e: e.tensor_single_scalar(
            out=mskt[:], in_=ltile[:], scalar=mx[:, TOPK - 1:TOPK], op=ALU.is_ge),
            wait=now(s_act))
        p.emit("vector", lambda e: e.tensor_tensor(out=expv[:], in0=expv[:],
                                                   in1=mskt[:], op=ALU.mult))
        p.emit("vector", lambda e: e.reduce_sum(scal[:, 10:11], expv[:], axis=AXX))
        p.emit("vector", lambda e: e.reciprocal(out=scal[:, 10:11],
                                                in_=scal[:, 10:11]))
        p.emit("vector", lambda e: e.tensor_scalar(
            out=wtok[:], in0=expv[:], scalar1=scal[:, 10:11], scalar2=None,
            op0=ALU.mult))
        p.emit("tensor", lambda e: e.transpose(pf[6][0:NE, 0:128], wtok[:], ident_f),
               wait=now(s_dve), inc=[(s_pe, 1)])
        p.emit("vector", lambda e, m=m: e.tensor_copy(
            out=wfeat[0:NE, m * 128:(m + 1) * 128], in_=pf[6][0:NE, 0:128]),
            wait=now(s_pe), inc=[(s_dve, 1)])
    mark("wfeat_done", s_dve)

    for _b in range(4):
        mark(f"exb_{_b}", s_dve)
    mark("dpb_4", s_dve)
    mark("dpb_5", s_dve)
    mark("wbc_evac", s_dve)
    for e_ in range(ne_run):
        g = e_ // G
        eg = e_ % G
        wsl = e_ % 2
        p.emit("gpsimd", lambda e, e_=e_: e.dma_start(
            out=wrow[0:1, :], in_=wfeat[e_:e_ + 1, :]),
            wait=W("wfeat_done") + W(f"wk1_{e_ - 1}"), inc=[(s_wr, 16)])
        ldr = now(s_wr)
        for half in range(2):
            wv_ = (list(ldr) if half == 0 else []) + W("wbc_evac")
            p.emit("tensor", lambda e, half=half: e.matmul(
                pf[6][:], ones_row_b,
                wrow[0:1, half * 512:(half + 1) * 512],
                start=True, stop=True), wait=wv_, inc=[(s_pe, 1)])
            p.emit("scalar", lambda e, wsl=wsl, half=half: e.copy(
                out=wbc[:, wsl * TOK + half * 512: wsl * TOK + (half + 1) * 512],
                in_=pf[6][:]), wait=now(s_pe) +
                (W("shint_last") if e_ == 0 and half == 0 else []),
                inc=[(s_act, 1)])
            mark("wbc_evac", s_act)
        mark(f"wk1_{e_}", s_pe)
        if e_ == 0:
            pass

        for ps in range(4):
            for gi in range(2):
                m_ = ps + 4 * gi
                ci = e_ * 8 + ps * 2 + gi
                slot = ci % 3
                p.emit("gpsimd", lambda e, e_=e_, m_=m_, slot=slot: e.dma_start(
                    out=gurv[:, slot, :, :], in_=d_gu[e_, m_]),
                    wait=W(f"gusl_{slot}"), inc=[(s_gu[slot], 16)])
            for tc in range(2):
                cg = e_ * 8 + ps * 2
                sg_, su_ = cg % 3, (cg + 1) % 3
                for gi2, b, sl_ in ((0, tc, sg_), (1, 2 + tc, su_)):
                    wv_ = (now(s_gu[sl_]) if tc == 0 else []) + W(f"exb_{b}")
                    def gmm(e, sl_=sl_, tc=tc, b=b):
                        last = None
                        for c in range(KC):
                            last = e.matmul(pf[b][:], gurv[:, sl_, c, :],
                                            r2v[:, c, tc * 512:(tc + 1) * 512],
                                            start=(c == 0), stop=(c == KC - 1))
                        return last
                    p.emit("tensor", gmm, wait=wv_, inc=[(s_pe, 1)])
                mark(f"gusl_{sg_}", s_pe)
                mark(f"gusl_{su_}", s_pe)
                p.emit("scalar", lambda e, tc=tc: e.activation(
                    out=sgt[:, tc * 512:(tc + 1) * 512], in_=pf[tc][:], func=AF.Silu),
                    wait=now(s_pe) + W(f"sgt_{tc}"), inc=[(s_act, 1)])
                mark(f"exb_{tc}", s_act)
                p.emit("vector", lambda e, tc=tc: e.tensor_tensor(
                    out=itt[:, tc * 512:(tc + 1) * 512], in0=pf[2 + tc][:],
                    in1=sgt[:, tc * 512:(tc + 1) * 512], op=ALU.mult),
                    wait=now(s_act) + (W(f"dppe_{g - 1}")
                                       if eg == 0 and ps == 0 and tc == 0 else []))
                p.emit("vector", lambda e, tc=tc, eg=eg, ps=ps, wsl=wsl:
                       e.tensor_tensor(
                           out=interv[:, eg * 4 + ps, tc * 512:(tc + 1) * 512],
                           in0=itt[:, tc * 512:(tc + 1) * 512],
                           in1=wbc[:, wsl * TOK + tc * 512: wsl * TOK + (tc + 1) * 512],
                           op=ALU.mult))
                mark(f"exb_{2 + tc}", s_dve)
                mark(f"sgt_{tc}", s_dve)
        if e_ == 0:
            mark("shint_last_unused", s_dve)

        if eg == G - 1:
            mark(f"inter_{g}", s_dve)
            for hc in range(KC):
                dsl = hc % 2
                p.emit("gpsimd", lambda e, hc=hc, g=g, dsl=dsl: e.dma_start(
                    out=dprv[:, dsl], in_=d_dp[hc, :, g * G:(g + 1) * G]),
                    wait=W(f"dpsl_{dsl}"), inc=[(s_dp[dsl], 16)])
                ldw = now(s_dp[dsl])
                for tc in range(2):
                    b = 4 + (hc * 2 + tc) % 2
                    wv_ = W(f"inter_{g}") + (list(ldw) if tc == 0 else []) + \
                          W(f"dpb_{b}")
                    def pmm(e, dsl=dsl, tc=tc, b=b):
                        last = None
                        for ee in range(G):
                            for ic in range(4):
                                last = e.matmul(
                                    pf[b][:], dprv[:, dsl, ee, ic, :],
                                    interv[:, ee * 4 + ic, tc * 512:(tc + 1) * 512],
                                    start=(ee == 0 and ic == 0),
                                    stop=(ee == G - 1 and ic == 3))
                        return last
                    p.emit("tensor", pmm, wait=wv_, inc=[(s_pe, 1)])
                    p.emit("vector", lambda e, hc=hc, tc=tc, b=b: e.tensor_tensor(
                        out=oav[:, hc, tc * 512:(tc + 1) * 512], in0=pf[b][:],
                        in1=oav[:, hc, tc * 512:(tc + 1) * 512], op=ALU.add),
                        wait=now(s_pe), inc=[(s_dve, 1)])
                    mark(f"dpb_{b}", s_dve)
                mark(f"dpsl_{dsl}", s_pe)
            mark(f"dppe_{g}", s_pe)
    mark("moe_done", s_dve)

    if dumps:
        dbg = {"r2": r2, "wfeat": wfeat, "g1b": g1b, "wtok": wtok, "ltile": ltile, "mx": mx, "expv": expv, "cur": cur,
               "inter": inter, "wbc": wbc, "sqb": sqb, "x2d": x2}
        for i, name in enumerate(dumps):
            tsr = dbg[name]
            shp = [int(x) for x in tsr[:].shape]
            dt_ = tsr[:].dtype
            od = nc.dram_tensor(f"dump_{name}", shp, dt_, kind="ExternalOutput").ap()
            p.emit("sync", lambda e, od=od, tsr=tsr: e.dma_start(out=od, in_=tsr[:]),
                   wait=W("moe_done") + now(s_dve, s_act, s_pe), inc=[(s_st, 16)])

    for t in range(8):
        osl = t % 2
        for g4 in range(4):
            b = g4 % 2
            wv_ = (W("moe_done") if t == 0 and g4 == 0 else []) + W(f"otr_{b}")
            def otr(e, t=t, g4=g4, b=b):
                last = None
                for u in range(4):
                    gg = g4 * 4 + u
                    last = e.transpose(pf[b][:, u * 128:(u + 1) * 128],
                                       oav[:, gg, t * 128:(t + 1) * 128], ident_f)
                return last
            p.emit("tensor", otr, wait=wv_, inc=[(s_pe, 1)])
            p.emit("vector", lambda e, g4=g4, b=b, osl=osl: e.tensor_copy(
                out=outstg[:, osl * HID + g4 * 512: osl * HID + (g4 + 1) * 512],
                in_=pf[b][:]),
                wait=now(s_pe) + ([(s_so[osl], 16 * (t // 2))] if t >= 2 and g4 == 0 else []),
                inc=[(s_dve, 1)])
            mark(f"otr_{b}", s_dve)
        p.emit("sync", lambda e, t=t, osl=osl: e.dma_start(
            out=d_out[t * 128:(t + 1) * 128, :],
            in_=outstg[:, osl * HID:(osl + 1) * HID]),
            wait=now(s_dve), inc=[(s_so[osl], 16)])
    p.emit("sync", lambda e, v0=s_so[0].val, v1=s_so[1].val:
           (e.wait_ge(s_so[0].h, v0), e.wait_ge(s_so[1].h, v1))[-1])

    return nc, p, dram


def _consts():
    cbv = np.zeros((128, 256), ml_dtypes.bfloat16)
    cbv[:, 0:128] = np.eye(128, dtype=ml_dtypes.bfloat16)
    cbv[:, 128:256] = 1
    cfv = np.zeros((128, 772), np.float32)
    cfv[:, 0:128] = np.eye(128, dtype=np.float32)
    cfv[:, 128:256] = 1
    cfv[:, 256:768] = (np.arange(512, dtype=np.float32)[None, :]
                       - np.arange(128, dtype=np.float32)[:, None])
    cfv[:, 768:769] = 1e-5
    return cbv, cfv


def _tau_for_core(s):
    tau = np.zeros(N_TAU, np.float32)
    for (qc, kt, ti) in SCORE_CHUNKS:
        if ti is None:
            continue
        tau[ti] = (kt * 128 - qc * 512) if kt < 8 else (BIG if s == 0 else -BIG)
    return np.ascontiguousarray(np.broadcast_to(tau[None, :], (128, N_TAU)))


def prep_weights(inputs):
    d = {}
    def qkvperm(w):
        return np.ascontiguousarray(
            np.asarray(w, np.float32).reshape(16, 128, 16, 128).transpose(0, 3, 2, 1))
    d["wq"] = qkvperm(inputs["L0_attn_Wq"])
    d["wk"] = qkvperm(inputs["L0_attn_Wk"])
    d["wv"] = qkvperm(inputs["L0_attn_Wv"])
    d["wo"] = np.ascontiguousarray(
        np.asarray(inputs["L0_attn_Wo"], np.float32).reshape(2048, 16, 128)
        .transpose(2, 1, 0))
    d["gamma"] = np.ascontiguousarray(
        np.asarray(inputs["L0_attn_gamma"], np.float32).reshape(128, 1))
    d["beta"] = np.ascontiguousarray(
        np.asarray(inputs["L0_attn_beta"], np.float32).reshape(128, 1))
    gu = np.asarray(inputs["L0_ffn_gu_w"], np.float32)
    d["gu"] = np.ascontiguousarray(
        gu.reshape(64, 8, 128, 16, 128).transpose(0, 1, 4, 3, 2))
    dp = np.asarray(inputs["L0_ffn_dp_w"], np.float32)
    d["dp"] = np.ascontiguousarray(
        dp.reshape(64, 16, 128, 4, 128).transpose(1, 4, 0, 3, 2))
    d["gatew"] = np.ascontiguousarray(
        np.asarray(inputs["L0_ffn_gate_w"], np.float32).reshape(64, 16, 128)
        .transpose(2, 1, 0))
    def shperm(w):
        return np.asarray(w, np.float32).reshape(4, 128, 16, 128).transpose(0, 3, 2, 1)
    d["shg"] = np.ascontiguousarray(np.stack(
        [shperm(inputs["L0_ffn_sh_gate_w"]), shperm(inputs["L0_ffn_sh_up_w"])], axis=0))
    d["shd"] = np.ascontiguousarray(
        np.asarray(inputs["L0_ffn_sh_down_w"], np.float32)
        .reshape(4, 4, 128, 4, 128).transpose(0, 4, 1, 3, 2))
    d["shg1"] = np.ascontiguousarray(
        np.asarray(inputs["L0_ffn_sh_gate1_w"], np.float32).reshape(16, 128)
        .transpose(1, 0)).reshape(128, 16, 1)
    d["consts_b"], d["consts_f"] = _consts()
    return d


def kernel(**inputs):
    x = np.asarray(inputs["L0_attn_hidden_states"], np.float32)
    shared = prep_weights(inputs)
    nc, p, dram = build_program("full")
    p.finish()
    in_maps = []
    for c in range(8):
        b, s = c // 2, c % 2
        own = x[b, s * TOK:(s + 1) * TOK]
        other = x[b, (1 - s) * TOK:(2 - s) * TOK]
        m = dict(shared)
        m["x_seq"] = np.ascontiguousarray(np.concatenate([own, other], axis=0))
        m["x_own"] = np.ascontiguousarray(own)
        m["tau"] = _tau_for_core(s)
        in_maps.append(m)
    res = run_bass_kernel_spmd(nc, in_maps, core_ids=list(range(8)))
    out = np.empty((4, 2048, 2048), np.float32)
    for c in range(8):
        b, s = c // 2, c % 2
        out[b, s * TOK:(s + 1) * TOK] = res.results[c]["out"]
    return out


# version 07 (final)
# speedup vs baseline: 1.0297x; 1.0192x over previous
import numpy as np
import ml_dtypes
import concourse.bass as bass
import concourse.mybir as mybir
from concourse.bass_utils import run_bass_kernel_spmd

F32 = mybir.dt.float32
BF16 = mybir.dt.bfloat16
AF = mybir.ActivationFunctionType
ALU = mybir.AluOpType
AXX = mybir.AxisListType.X

HID = 2048
T = 2048
TOK = 1024
NH = 16
NE = 64
TOPK = 8
G = 4
KC = HID // 128
EPS = 1e-5
BIG = 1.0e9
INVH = 1.0 / HID
INVD = 1.0 / 128.0
QSC = 128.0 ** -0.5

SCORE_CHUNKS = []
_ti = 0
_sj = 0
for _qc in range(2):
    for _kt in range(16):
        if _kt < 8:
            if _kt * 128 > _qc * 512 + 511:
                continue
            if _kt * 128 + 127 <= _qc * 512:
                SCORE_CHUNKS.append((_qc, _kt, None, None))
            else:
                SCORE_CHUNKS.append((_qc, _kt, _ti, None)); _ti += 1
        else:
            SCORE_CHUNKS.append((_qc, _kt, None, _sj)); _sj += 1
N_TAU = _ti
N_KSC = _sj
assert len(SCORE_CHUNKS) == 28 and N_TAU == 8 and N_KSC == 16


class Ctr:
    __slots__ = ("h", "name", "val")

    def __init__(self, h, name):
        self.h, self.name, self.val = h, name, 0


class Prog:

    def __init__(self, nc):
        self.nc = nc
        self.ev = {e: [] for e in ("sync", "gpsimd", "scalar", "vector", "tensor")}
        self.vctr = None

    def sem(self, name):
        return Ctr(self.nc.alloc_semaphore(name), name)

    def emit(self, eng, fn, wait=(), inc=(), selfw=True):
        wait = list(wait)
        inc = list(inc)
        if eng == "vector" and self.vctr is not None and selfw:
            wait.append((self.vctr, self.vctr.val))
            if not any(s is self.vctr for s, _ in inc):
                inc.append((self.vctr, 1))
        incs = []
        for s, n in inc:
            s.val += n
            incs.append((s, n))
        seen = {}
        for s, v in wait:
            if v > 0 and (id(s) not in seen or seen[id(s)][1] < v):
                seen[id(s)] = (s, v)
        self.ev[eng].append((fn, list(seen.values()), incs))

    def finish(self):
        with self.nc.Block() as blk:
            def mk(name):
                def body(eng):
                    for fn, waits, incs in self.ev[name]:
                        for s, v in waits:
                            eng.wait_ge(s.h, v)
                        last = fn(eng)
                        for s, n in incs:
                            last = last.then_inc(s.h, n)
                return body
            blk.sync(mk("sync"))
            blk.gpsimd(mk("gpsimd"))
            blk.scalar(mk("scalar"))
            blk.vector(mk("vector"))
            blk.tensor(mk("tensor"))


def build_program(mode="full", ne_run=NE, dumps=()):
    nc = bass.Bass("TRN2", target_bir_lowering=False, debug=False)
    p = Prog(nc)
    do_attn = mode in ("full", "attn")
    do_moe = mode in ("full", "moe")
    dram = {}

    def din(name, shape, dtype=F32):
        dram[name] = nc.dram_tensor(name, list(shape), dtype, kind="ExternalInput").ap()
        return dram[name]

    def dout(name, shape, dtype=F32):
        dram[name] = nc.dram_tensor(name, list(shape), dtype, kind="ExternalOutput").ap()
        return dram[name]

    if do_attn:
        d_xseq = din("x_seq", (T, HID))
        d_wq = din("wq", (NH, 128, KC, 128))
        d_wk = din("wk", (NH, 128, KC, 128))
        d_wv = din("wv", (NH, 128, KC, 128))
        d_wo = din("wo", (128, KC, HID))
        d_gam = din("gamma", (128, 1))
        d_bet = din("beta", (128, 1))
        d_tau = din("tau", (128, N_TAU))
        d_ksc = din("ksc", (128, N_KSC))
        d_xown = din("x_own", (TOK, HID))
    if mode == "moe":
        d_x2in = din("x2_in", (TOK, HID))
    if do_moe:
        d_gu = din("gu", (NE, 8, 128, KC, 128))
        d_dp = din("dp", (KC, 128, NE, 4, 128))
        d_gatew = din("gatew", (128, KC, NE))
        d_shg = din("shg", (2, 4, 128, KC, 128))
        d_shd = din("shd", (4, 128, 4, 4, 128))
        d_shg1 = din("shg1", (128, KC, 1))
    d_cb = din("consts_b", (128, 256), BF16)
    d_cf = din("consts_f", (128, 772), F32)
    d_out = dout("x2_out" if mode == "attn" else "out", (TOK, HID))

    cb = nc.alloc_sbuf_tensor("cb", [128, 256], BF16)
    cf = nc.alloc_sbuf_tensor("cf", [128, 772], F32)
    ident_b, ones_col_b = cb[:, 0:128], cb[:, 128:129]
    ones_row_b = cb[0:1, 128:256]
    ident_f, ones_col_f = cf[:, 0:128], cf[:, 128:129]
    ones_row_f = cf[0:1, 128:256]
    iota_fp = cf[:, 256:768]
    epsv = cf[:, 768:769]
    epsr = cf[0:1, 768:769]
    scal = nc.alloc_sbuf_tensor("scal", [128, 16], F32)
    if do_attn:
        gam = nc.alloc_sbuf_tensor("gam", [128, 1], F32)
        bet = nc.alloc_sbuf_tensor("bet", [128, 1], F32)
        taus = nc.alloc_sbuf_tensor("taus", [128, N_TAU], F32)
        kscs = nc.alloc_sbuf_tensor("kscs", [128, N_KSC], F32)
    if do_moe:
        gwb = nc.alloc_sbuf_tensor("gwb", [128, KC * NE], BF16)
        gwv = gwb[:].rearrange("p (c n) -> p c n", c=KC)
        shg1b = nc.alloc_sbuf_tensor("shg1b", [128, KC], BF16)
        ltile = nc.alloc_sbuf_tensor("ltile", [128, NE], F32)
        cur = nc.alloc_sbuf_tensor("cur", [128, NE], F32)
        mskt = nc.alloc_sbuf_tensor("mskt", [128, NE], F32)
        expv = nc.alloc_sbuf_tensor("expv", [128, NE], F32)
        mx = nc.alloc_sbuf_tensor("mx", [128, TOPK], F32)
        wtok = nc.alloc_sbuf_tensor("wtok", [128, NE], F32)
        wfeat = nc.alloc_sbuf_tensor("wfeat", [64, TOK], F32)
        wrow = nc.alloc_sbuf_tensor("wrow", [1, TOK], BF16)
        g1r = nc.alloc_sbuf_tensor("g1r", [1, TOK], F32)
        g1b = nc.alloc_sbuf_tensor("g1b", [128, TOK], F32)
        oinsp = nc.alloc_sbuf_tensor("oinsp", [128, 512], F32)


    SB_END = 16512 + 212832
    _free0 = nc.sbuf_bytes_remaining
    BA = (SB_END - _free0 + 127) // 128 * 128

    def at(name, shape, dtype, off):
        nb = int(np.prod(shape[1:])) * mybir.dt.size(dtype)
        assert BA + off + nb <= SB_END, f"{name}: {BA + off + nb} > {SB_END}"
        return nc.alloc_sbuf_tensor_at(name, list(shape), dtype, offset=BA + off)

    if do_attn:
        r1 = at("r1", (128, KC * T), BF16, 0)
        r1v = r1[:].rearrange("p (c t) -> p c t", c=KC)
    x2 = at("x2", (128, 8 * HID), F32, 0)
    if do_moe:
        inter = at("inter", (128, G * 4 * TOK), BF16, 0)
        interv = inter[:].rearrange("p (i t) -> p i t", i=G * 4)
        sgt = at("sgt", (128, 2 * 512), BF16, 32768)
        itt = at("itt", (128, 2 * 512), BF16, 32768 + 2048)
        exsp = at("exsp", (128, 512), F32, 32768 + 4096)
    RB = 65536
    if do_attn:
        xin = at("xin", (128, 3 * HID), F32, RB)
        sqb_l1 = at("sqb_l1", (128, HID), BF16, RB + 24576)
        ofeat = at("ofeat", (128, KC * TOK), BF16, RB)
        ofv = ofeat[:].rearrange("p (c t) -> p c t", c=KC)
        y0qT = at("y0qT", (128, 8 * 128), BF16, RB + 32768)
        y0kT = at("y0kT", (128, 16 * 128), BF16, RB + 32768 + 2048)
        scol = at("scol", (128, 32), F32, RB + 32768 + 6144)
        qcol = at("qcol", (128, 32), F32, RB + 32768 + 6272)
        tcol = at("tcol", (128, 32), F32, RB + 32768 + 6400)
        y2d = at("y2d", (128, 24 * 128), BF16, RB + 32768 + 6528)
        wob = at("wob", (128, 2 * KC * 512), BF16, RB + 32768)
        wobv = [wob[:, i * KC * 512:(i + 1) * KC * 512].rearrange(
            "p (c n) -> p c n", c=KC) for i in range(2)]
    if do_moe:
        sqb = at("sqb", (128, HID), BF16, RB + 32768)
        oacc = at("oacc", (128, KC * TOK), F32, RB)
        oav = oacc[:].rearrange("p (c t) -> p c t", c=KC)
    RC = 131072
    if do_attn:
        y0k2 = [at(f"y0k{i}", (128, T), BF16, RC + i * 4096) for i in range(2)]
        vtok2 = [at(f"vtok{i}", (128, 16 * 128), BF16, RC + 8192 + i * 4096)
                 for i in range(2)]
        y0q2 = [at(f"y0q{i}", (128, TOK), BF16, RC + 16384 + i * 2048)
                for i in range(2)]
        vft = at("vft", (128, T), BF16, RC + 20480)
        ssq = at("ssq", (128, 4 * 512), BF16, RC + 24576)
        zbs = at("zbs", (128, 512), F32, RB + 32768 + 29696)
        nrmt = at("nrmt", (128, 512), BF16, RB + 32768 + 31744)
        zrow = at("zrow", (1, 512), F32, RC + 28672)
    if do_moe:
        r2 = at("r2", (128, KC * TOK), BF16, RC)
        r2v = r2[:].rearrange("p (c t) -> p c t", c=KC)
    RE = 163840
    if do_attn:
        wqkv = at("wqkv", (128, 2 * 3 * KC * 128), BF16, RE)
        wvv = wqkv[:].rearrange("p (s w c j) -> p s w c j", s=2, w=3, c=KC)
        xores = at("xores", (128, 2 * 512), F32, RE)
    if do_moe:
        gur = at("gur", (128, 3 * KC * 128), BF16, RE)
        gurv = gur[:].rearrange("p (s c j) -> p s c j", s=3, c=KC)
        dpr = at("dpr", (128, 2 * 16 * 128), BF16, RE + 12288)
        dprv = dpr[:].rearrange("p (s e i j) -> p s e i j", s=2, e=G, i=4)
        dprs = dpr[:].rearrange("p (s h i j) -> p s h i j", s=2, h=4, i=4)
        shsg = at("shsg", (128, 4 * 512), BF16, RE + 12288 + 8192)
        shsv = shsg[:].rearrange("p (m t) -> p m t", m=4)
        wbc = at("wbc", (128, 2 * TOK), BF16, RE + 12288 + 8192)
    outstg = at("outstg", (128, 2 * HID), F32, RE)

    pf = [nc.alloc_psum_tensor(f"pf{i}", [128, 512], F32) for i in range(7)]
    pbf = nc.alloc_psum_tensor("pbf", [128, 1024], BF16)

    s_pe = p.sem("s_pe")
    s_act = p.sem("s_act")
    s_dve = p.sem("s_dve")
    p.vctr = s_dve
    s_wqkv = [p.sem("s_wqkv0"), p.sem("s_wqkv1")]
    s_wo = [p.sem("s_wo0"), p.sem("s_wo1")]
    s_gu = [p.sem(f"s_gu{i}") for i in range(3)]
    s_dp = [p.sem(f"s_dp{i}") for i in range(2)]
    s_one = p.sem("s_one")
    s_x = p.sem("s_x")
    s_xr = [p.sem(f"s_xr{i}") for i in range(3)]
    s_xo = [p.sem(f"s_xo{i}") for i in range(2)]
    s_so = [p.sem(f"s_so{i}") for i in range(2)]
    s_st = p.sem("s_st")
    s_wr = p.sem("s_wr")

    chk = {}

    def mark(key, ctr):
        chk[key] = (ctr, ctr.val)

    def W(*keys):
        return [chk[k] for k in keys if k in chk]

    def now(*ctrs):
        return [(c, c.val) for c in ctrs]

    def soft():
        return now(s_pe, s_act, s_dve)

    p.emit("sync", lambda e: e.dma_start(out=cb[:], in_=d_cb), inc=[(s_x, 16)])
    p.emit("sync", lambda e: e.dma_start(out=cf[:], in_=d_cf), inc=[(s_x, 16)])
    if do_attn:
        p.emit("sync", lambda e: e.dma_start(out=gam[:], in_=d_gam), inc=[(s_x, 16)])
        p.emit("sync", lambda e: e.dma_start(out=bet[:], in_=d_bet), inc=[(s_x, 16)])
        p.emit("sync", lambda e: e.dma_start(out=taus[:], in_=d_tau), inc=[(s_x, 16)])
        p.emit("sync", lambda e: e.dma_start(out=kscs[:], in_=d_ksc), inc=[(s_x, 16)])
    if do_moe:
        p.emit("gpsimd", lambda e: e.dma_start(out=gwv, in_=d_gatew), inc=[(s_one, 16)])
        p.emit("gpsimd", lambda e: e.dma_start(
            out=shg1b[:].rearrange("p (c n) -> p c n", n=1), in_=d_shg1),
            inc=[(s_one, 16)])
    base_w = now(s_x, s_one)

    def ln_tile(x_ap, out_ap, extra_wait=()):
        p.emit("vector", lambda e: e.reduce_sum(scal[:, 0:1], x_ap, axis=AXX),
               wait=tuple(extra_wait), inc=[(s_dve, 1)])
        for cc in range(4):
            p.emit("scalar", lambda e, cc=cc: e.activation(
                out=pf[cc % 2][:], in_=x_ap[:, cc * 512:(cc + 1) * 512],
                func=AF.Square, accum_out=scal[:, 4 + cc:5 + cc]),
                wait=list(extra_wait) + now(s_dve, s_pe) + now(s_act),
                inc=[(s_act, 1)])
        def mid1(e):
            e.reduce_sum(scal[:, 1:2], scal[:, 4:8], axis=AXX)
            return e.tensor_scalar(out=scal[:, 0:1], in0=scal[:, 0:1], scalar1=INVH,
                                   scalar2=None, op0=ALU.mult)
        p.emit("vector", mid1, wait=now(s_act))
        def mid2(e):
            e.tensor_scalar(out=scal[:, 1:2], in0=scal[:, 1:2], scalar1=INVH,
                            scalar2=None, op0=ALU.mult)
            return e.tensor_tensor(out=scal[:, 2:3], in0=scal[:, 0:1],
                                   in1=scal[:, 0:1], op=ALU.mult)
        p.emit("vector", mid2)
        p.emit("vector", lambda e: e.tensor_tensor(out=scal[:, 1:2], in0=scal[:, 1:2],
                                                   in1=scal[:, 2:3], op=ALU.subtract))
        p.emit("scalar", lambda e: e.activation(out=scal[:, 3:4], in_=scal[:, 1:2],
                                                func=AF.Sqrt, bias=epsv),
               wait=now(s_dve), inc=[(s_act, 1)])
        p.emit("vector", lambda e: e.reciprocal(out=scal[:, 3:4], in_=scal[:, 3:4]),
               wait=now(s_act))
        p.emit("vector", lambda e: e.tensor_scalar(
            out=out_ap, in0=x_ap, scalar1=scal[:, 0:1], scalar2=scal[:, 3:4],
            op0=ALU.subtract, op1=ALU.mult))

    def transpose_tile(src_ap, dst_fn, nchunk, extra=()):
        for j0 in range(0, nchunk, 8):
            jn = min(8, nchunk - j0)
            def tr(e, j0=j0, jn=jn):
                last = None
                for j in range(jn):
                    last = e.transpose(pbf[:, j * 128:(j + 1) * 128],
                                       src_ap[:, (j0 + j) * 128:(j0 + j + 1) * 128],
                                       ident_b)
                return last
            p.emit("tensor", tr, wait=now(s_dve) + list(extra), inc=[(s_pe, 1)])
            def ev(e, j0=j0, jn=jn):
                last = None
                for j in range(jn):
                    last = e.tensor_copy(out=dst_fn(j0 + j),
                                         in_=pbf[:, j * 128:(j + 1) * 128])
                return last
            p.emit("vector", ev, wait=now(s_pe), inc=[(s_dve, 1)])

    if do_attn:
        for t in range(16):
            slot = t % 3
            xt = xin[:, slot * HID:(slot + 1) * HID]
            p.emit("sync", lambda e, xt=xt, t=t: e.dma_start(
                out=xt, in_=d_xseq[t * 128:(t + 1) * 128, :]),
                wait=W(f"l1fin_{t - 3}"), inc=[(s_xr[slot], 16)])
            ln_tile(xt, sqb_l1[:], extra_wait=now(s_xr[slot]) + (base_w if t == 0 else []))
            mark(f"l1fin_{t}", s_dve)
            transpose_tile(sqb_l1[:], lambda j, t=t: r1v[:, j, t * 128:(t + 1) * 128], KC)
        mark("r1_done", s_dve)

        def stageA1(h):
            sl = h % 2
            for wi, dsrc in enumerate((d_wq, d_wk, d_wv)):
                p.emit("gpsimd", lambda e, wi=wi, dsrc=dsrc, sl=sl, h=h: e.dma_start(
                    out=wvv[:, sl, wi, :, :], in_=dsrc[h]),
                    wait=W(f"hproj_{h - 2}") if wi == 0 else [],
                    inc=[(s_wqkv[sl], 16)])
            ldw = now(s_wqkv[sl])

            def proj(dst_fn, wsel, nck, fm, first):
                for q2 in range(nck):
                    b = q2 % 2
                    wv_ = list(ldw) if q2 == 0 else []
                    if first and q2 == 0:
                        wv_ += W("r1_done") + now(s_act)
                    wv_ += W(f"hpb_{b}")
                    def mm(e, q2=q2, b=b, wsel=wsel, sl=sl):
                        last = None
                        for c in range(KC):
                            last = e.matmul(pf[b][:], wvv[:, sl, wsel, c, :],
                                            r1v[:, c, q2 * 512:(q2 + 1) * 512],
                                            start=(c == 0), stop=(c == KC - 1))
                        return last
                    p.emit("tensor", mm, wait=wv_, inc=[(s_pe, 1)])
                    def evac(e, q2=q2, b=b, dst_fn=dst_fn, fm=fm):
                        dst = dst_fn(q2)
                        if fm:
                            return e.activation(out=dst, in_=pf[b][:], func=AF.Identity,
                                                scale=gam[:], bias=bet[:])
                        return e.copy(out=dst, in_=pf[b][:])
                    p.emit("scalar", evac, wait=now(s_pe) + now(s_act),
                           inc=[(s_act, 1)])
                    mark(f"hpb_{b}", s_act)
            par = h % 2
            proj(lambda q2: y0q2[par][:, q2 * 512:(q2 + 1) * 512], 0, 2, True, h == 0)
            proj(lambda q2: y0k2[par][:, q2 * 512:(q2 + 1) * 512], 1, 4, True, False)
            proj(lambda q2: vft[:, q2 * 512:(q2 + 1) * 512], 2, 4, False, False)
            mark(f"hproj_{h}", s_pe)
            transpose_tile(vft[:], lambda j, par=par: vtok2[par][:, j * 128:(j + 1) * 128], 16, extra=now(s_act))

            ntq, ntk = 8, 16
            def tbatch(src_ap, dst_ap, nt, extra=()):
                for j0 in range(0, nt, 8):
                    jn = min(8, nt - j0)
                    def tr(e, j0=j0, jn=jn, src_ap=src_ap):
                        last = None
                        for j in range(jn):
                            last = e.transpose(pbf[:, j * 128:(j + 1) * 128],
                                               src_ap[:, (j0 + j) * 128:(j0 + j + 1) * 128],
                                               ident_b)
                        return last
                    p.emit("tensor", tr, wait=now(s_dve) + list(extra), inc=[(s_pe, 1)])
                    def ev(e, j0=j0, jn=jn, dst_ap=dst_ap):
                        last = None
                        for j in range(jn):
                            last = e.tensor_copy(
                                out=dst_ap[:, (j0 + j) * 128:(j0 + j + 1) * 128],
                                in_=pbf[:, j * 128:(j + 1) * 128])
                        return last
                    p.emit("vector", ev, wait=now(s_pe), inc=[(s_dve, 1)])
            tbatch(y0q2[par][:], y0qT[:], ntq, extra=now(s_act))
            tbatch(y0k2[par][:], y0kT[:], ntk, extra=now(s_act))
            p.emit("vector", lambda e: e.reduce_sum(
                scol[:, 16:24], y0qT[:].rearrange("p (t n) -> p t n", n=128), axis=AXX))
            p.emit("vector", lambda e: e.reduce_sum(
                scol[:, 0:16], y0kT[:].rearrange("p (t n) -> p t n", n=128), axis=AXX))
            for t in range(24):
                ya = y0kT if t < 16 else y0qT
                off = t if t < 16 else t - 16
                p.emit("scalar", lambda e, ya=ya, off=off, t=t: e.activation(
                    out=y2d[:, t * 128:(t + 1) * 128],
                    in_=ya[:, off * 128:(off + 1) * 128], func=AF.Square,
                    accum_out=qcol[:, t:t + 1]),
                    wait=(now(s_act) + now(s_dve)) if t == 0 else [],
                    inc=[(s_act, 1)])
            p.emit("vector", lambda e: e.tensor_scalar(
                out=scol[:, 0:24], in0=scol[:, 0:24], scalar1=INVD, scalar2=None,
                op0=ALU.mult), wait=now(s_act))
            p.emit("vector", lambda e: e.tensor_tensor(
                out=tcol[:, 0:24], in0=scol[:, 0:24], in1=scol[:, 0:24], op=ALU.mult))
            p.emit("vector", lambda e: e.scalar_tensor_tensor(
                out=qcol[:, 0:24], in0=qcol[:, 0:24], scalar=INVD, in1=tcol[:, 0:24],
                op0=ALU.mult, op1=ALU.subtract))
            p.emit("scalar", lambda e: e.activation(
                out=qcol[:, 0:24], in_=qcol[:, 0:24], func=AF.Sqrt, bias=epsv),
                wait=now(s_dve), inc=[(s_act, 1)])
            p.emit("vector", lambda e: e.reciprocal(out=qcol[:, 0:24],
                                                    in_=qcol[:, 0:24]),
                   wait=now(s_act))
            p.emit("vector", lambda e: e.tensor_scalar(
                out=qcol[:, 16:24], in0=qcol[:, 16:24], scalar1=QSC, scalar2=None,
                op0=ALU.mult))
            p.emit("vector", lambda e: e.scalar_tensor_tensor(
                out=tcol[:, 0:24], in0=scol[:, 0:24], scalar=-1.0, in1=qcol[:, 0:24],
                op0=ALU.mult, op1=ALU.mult))
            chk["_fmrows"] = (s_dve, s_dve.val)

        def stageA2(h):
            par = h % 2
            ntq, ntk = 8, 16
            def tbatch(src_ap, dst_ap, nt):
                for j0 in range(0, nt, 8):
                    jn = min(8, nt - j0)
                    def tr(e, j0=j0, jn=jn, src_ap=src_ap):
                        last = None
                        for j in range(jn):
                            last = e.transpose(pbf[:, j * 128:(j + 1) * 128],
                                               src_ap[:, (j0 + j) * 128:(j0 + j + 1) * 128],
                                               ident_b)
                        return last
                    p.emit("tensor", tr, wait=now(s_dve), inc=[(s_pe, 1)])
                    def ev(e, j0=j0, jn=jn, dst_ap=dst_ap):
                        last = None
                        for j in range(jn):
                            last = e.tensor_copy(
                                out=dst_ap[:, (j0 + j) * 128:(j0 + j + 1) * 128],
                                in_=pbf[:, j * 128:(j + 1) * 128])
                        return last
                    p.emit("vector", ev, wait=now(s_pe), inc=[(s_dve, 1)])
            for t in range(24):
                ya = y0kT if t < 16 else y0qT
                off = t if t < 16 else t - 16
                p.emit("vector", lambda e, ya=ya, off=off, t=t: e.tensor_scalar(
                    out=ya[:, off * 128:(off + 1) * 128],
                    in0=ya[:, off * 128:(off + 1) * 128],
                    scalar1=qcol[:, t:t + 1], scalar2=tcol[:, t:t + 1],
                    op0=ALU.mult, op1=ALU.add),
                    wait=([chk["_fmrows"]] + W("backT_prev")) if t in (0, 16) else [],
                    selfw=(t == 0))
            tbatch(y0qT[:], y0q2[par][:], ntq)
            tbatch(y0kT[:], y0k2[par][:], ntk)
            mark(f"fmdone_{h}", s_dve)
            mark("backT_prev", s_pe)

        def stageB(h):
            par = h % 2
            qfm_, kfm_, vtok_ = y0q2[par], y0k2[par], vtok2[par]
            for q2 in range(2):
                lst = [c4 for c4 in SCORE_CHUNKS if c4[0] == q2]
                nl = len(lst)
                def emit_score(ci):
                    qc, kt, ti, scj = lst[ci]
                    sb_ = ci % 2
                    ssl = ssq[:, (ci % 4) * 512:(ci % 4 + 1) * 512]
                    wv_ = W(f"sqev_{sb_}")
                    if ci == 0:
                        wv_ += W(f"fmdone_{h}") + W(f"oev_{1 - q2}")
                    p.emit("tensor", lambda e, kt=kt, qc=qc, sb_=sb_, kfm_=kfm_,
                           qfm_=qfm_: e.matmul(
                        pf[2 + sb_][:], kfm_[:, kt * 128:(kt + 1) * 128],
                        qfm_[:, qc * 512:(qc + 1) * 512], start=True, stop=True),
                        wait=wv_, inc=[(s_pe, 1)])
                    p.emit("scalar", lambda e, sb_=sb_, ssl=ssl, scj=scj: e.activation(
                        out=ssl, in_=pf[2 + sb_][:], func=AF.Square,
                        scale=(kscs[:, scj:scj + 1] if scj is not None else 1.0)),
                        wait=now(s_pe) + W(f"avz_{ci % 4}"), inc=[(s_act, 1)])
                    mark(f"sqev_{sb_}", s_act)
                    if ti is not None:
                        p.emit("vector", lambda e, ssl=ssl, ti=ti:
                               e.scalar_tensor_tensor(out=ssl, in0=iota_fp,
                                                      scalar=taus[:, ti:ti + 1],
                                                      in1=ssl, op0=ALU.is_ge,
                                                      op1=ALU.mult),
                               wait=now(s_act), inc=[(s_dve, 1)])
                    mark(f"smask_{ci % 4}", s_dve if ti is not None else s_act)
                def emit_avz(ci):
                    qc, kt, ti, scj = lst[ci]
                    ssl = ssq[:, (ci % 4) * 512:(ci % 4 + 1) * 512]
                    mk = chk.get(f"smask_{ci % 4}")
                    def avz(e, kt=kt, q2=q2, ssl=ssl, first=(ci == 0),
                            last_=(ci == nl - 1), vtok_=vtok_):
                        e.matmul(pf[4 + q2][:], vtok_[:, kt * 128:(kt + 1) * 128],
                                 ssl, start=first, stop=last_, skip_group_check=True)
                        return e.matmul(pf[6][0:1, :], ones_col_b, ssl,
                                        start=first, stop=last_, skip_group_check=True)
                    wv_ = [mk] if mk else []
                    if ci == 0:
                        wv_ += W(f"oev_{q2}") + W("zread") + W("pf6_free")
                    p.emit("tensor", avz, wait=wv_, inc=[(s_pe, 1)])
                    mark(f"avz_{ci % 4}", s_pe)
                for ci in range(nl):
                    emit_score(ci)
                    if ci >= 2:
                        emit_avz(ci - 2)
                emit_avz(nl - 2)
                emit_avz(nl - 1)
                p.emit("vector", lambda e: e.tensor_scalar(
                    out=zrow[0:1, :], in0=pf[6][0:1, :], scalar1=EPS,
                    scalar2=None, op0=ALU.add), wait=now(s_pe))
                p.emit("vector", lambda e: e.reciprocal(out=zrow[0:1, :],
                                                        in_=zrow[0:1, :]))
                mark("zread", s_dve)
                mark("pf6_free", s_dve)
                p.emit("tensor", lambda e: e.matmul(
                    pf[2][:], ones_row_f, zrow[0:1, :], start=True, stop=True),
                    wait=now(s_dve) + W("sqev_0"), inc=[(s_pe, 1)])
                p.emit("scalar", lambda e: e.copy(out=zbs[:], in_=pf[2][:]),
                       wait=now(s_pe) + W(f"oev_{1 - q2}"), inc=[(s_act, 1)])
                p.emit("vector", lambda e, h=h, q2=q2: e.tensor_tensor(
                    out=ofv[:, h, q2 * 512:(q2 + 1) * 512], in0=pf[4 + q2][:],
                    in1=zbs[:], op=ALU.mult), wait=now(s_act), inc=[(s_dve, 1)])
                mark(f"oev_{q2}", s_dve)
            mark(f"hdone_{h}", s_dve)

        for h in range(NH):
            stageA1(h)
            if h >= 1:
                stageB(h - 1)
            stageA2(h)
        stageB(NH - 1)
        mark(f"hproj_{NH - 1}", s_pe)

        for n in range(4):
            wsl = n % 2
            p.emit("gpsimd", lambda e, n=n, wsl=wsl: e.dma_start(
                out=wobv[wsl], in_=d_wo[:, :, n * 512:(n + 1) * 512]),
                wait=(W(f"hdone_{NH - 1}") if n == 0 else W(f"wopg_{n - 2}")),
                inc=[(s_wo[wsl], 16)])
            ldw = now(s_wo[wsl])
            for m in range(8):
                xsl = m % 2
                p.emit("sync", lambda e, n=n, m=m, xsl=xsl: e.dma_start(
                    out=xores[:, xsl * 512:(xsl + 1) * 512],
                    in_=d_xown[m * 128:(m + 1) * 128, n * 512:(n + 1) * 512]),
                    wait=(W(f"hproj_{NH - 1}") if (n == 0 and m == 0) else []) +
                         W(f"xores_{xsl}"), inc=[(s_xo[xsl], 16)])
                ldx = now(s_xo[xsl])
                b = m % 2
                wv_ = (list(ldw) if m == 0 else []) + W(f"wob_{b}")
                if n == 0 and m < 2:
                    wv_ += W(f"hpb_{b}")
                def wmm(e, m=m, wsl=wsl, b=b):
                    last = None
                    for c in range(KC):
                        last = e.matmul(pf[b][:], ofv[:, c, m * 128:(m + 1) * 128],
                                        wobv[wsl][:, c, :], start=(c == 0),
                                        stop=(c == KC - 1))
                    return last
                p.emit("tensor", wmm, wait=wv_, inc=[(s_pe, 1)])
                p.emit("vector", lambda e, n=n, m=m, b=b, xsl=xsl: e.tensor_tensor(
                    out=x2[:, m * HID + n * 512: m * HID + (n + 1) * 512],
                    in0=pf[b][:], in1=xores[:, xsl * 512:(xsl + 1) * 512], op=ALU.add),
                    wait=now(s_pe) + ldx, inc=[(s_dve, 1)])
                mark(f"wob_{b}", s_dve)
                mark(f"xores_{xsl}", s_dve)
            mark(f"wopg_{n}", s_pe)
        mark("x2_done", s_dve)

        if mode == "attn":
            if dumps:
                dbg = {"r1": r1, "y0q": y0q2[1], "y0k": y0k2[1],
                       "vtok": vtok2[1], "ofeat": ofeat, "zbs": zbs, "vft": vft}
                for name in dumps:
                    tsr = dbg[name]
                    shp = [int(xx) for xx in tsr[:].shape]
                    od = nc.dram_tensor(f"dump_{name}", shp, tsr[:].dtype,
                                        kind="ExternalOutput").ap()
                    p.emit("sync", lambda e, od=od, tsr=tsr: e.dma_start(
                        out=od, in_=tsr[:]),
                        wait=W("x2_done") + now(s_dve, s_act, s_pe), inc=[(s_st, 16)])
            for t in range(8):
                p.emit("sync", lambda e, t=t: e.dma_start(
                    out=d_out[t * 128:(t + 1) * 128, :],
                    in_=x2[:, t * HID:(t + 1) * HID]),
                    wait=W("x2_done"), inc=[(s_st, 16)])
            p.emit("sync", lambda e, v=s_st.val: e.wait_ge(s_st.h, v))
            return nc, p, dram

    if mode == "moe":
        for t in range(8):
            p.emit("sync", lambda e, t=t: e.dma_start(
                out=x2[:, t * HID:(t + 1) * HID],
                in_=d_x2in[t * 128:(t + 1) * 128, :]), inc=[(s_x, 16)])
        p.emit("vector", lambda e: e.engine_nop(),
               wait=now(s_x) + base_w, inc=[(s_dve, 1)])
        mark("x2_done", s_dve)

    for t in range(8):
        ln_tile(x2[:, t * HID:(t + 1) * HID], sqb[:],
                extra_wait=(W("x2_done") + soft()) if t == 0 else ())
        transpose_tile(sqb[:], lambda j, t=t: r2v[:, j, t * 128:(t + 1) * 128], KC)
    mark("r2_done", s_dve)

    for tc in range(2):
        def g1mm(e, tc=tc):
            last = None
            for c in range(KC):
                last = e.matmul(pf[6][0:1, :], shg1b[:, c:c + 1],
                                r2v[:, c, tc * 512:(tc + 1) * 512],
                                start=(c == 0), stop=(c == KC - 1))
            return last
        p.emit("tensor", g1mm, wait=now(s_dve, s_act), inc=[(s_pe, 1)])
        p.emit("scalar", lambda e, tc=tc: e.activation(
            out=g1r[0:1, tc * 512:(tc + 1) * 512], in_=pf[6][0:1, :], func=AF.Sigmoid),
            wait=now(s_pe), inc=[(s_act, 1)])
        p.emit("tensor", lambda e, tc=tc: e.matmul(
            pf[6][:], ones_row_f, g1r[0:1, tc * 512:(tc + 1) * 512],
            start=True, stop=True), wait=now(s_act), inc=[(s_pe, 1)])
        p.emit("scalar", lambda e, tc=tc: e.copy(
            out=g1b[:, tc * 512:(tc + 1) * 512], in_=pf[6][:]),
            wait=now(s_pe), inc=[(s_act, 1)])
    mark("g1_done", s_act)

    for tc in range(2):
        for gi in range(2):
            for mc in range(4):
                ci = tc * 8 + gi * 4 + mc
                slot = ci % 2
                p.emit("gpsimd", lambda e, gi=gi, mc=mc, slot=slot: e.dma_start(
                    out=gurv[:, slot, :, :], in_=d_shg[gi, mc]),
                    wait=(W("x2_done") if ci == 0 else W(f"gusl_{slot}")),
                    inc=[(s_gu[slot], 16)])
                ldw = now(s_gu[slot])
                b = mc % 2 if gi == 0 else 2 + mc % 2
                wv_ = list(ldw) + W(f"shb_{b}")
                def smm(e, slot=slot, tc=tc, b=b):
                    last = None
                    for c in range(KC):
                        last = e.matmul(pf[b][:], gurv[:, slot, c, :],
                                        r2v[:, c, tc * 512:(tc + 1) * 512],
                                        start=(c == 0), stop=(c == KC - 1))
                    return last
                p.emit("tensor", smm, wait=wv_, inc=[(s_pe, 1)])
                mark(f"gusl_{slot}", s_pe)
                if gi == 0:
                    p.emit("scalar", lambda e, mc=mc, b=b: e.activation(
                        out=shsv[:, mc, :], in_=pf[b][:], func=AF.Silu),
                        wait=now(s_pe) + W(f"shsl_{mc}"), inc=[(s_act, 1)])
                    mark(f"shb_{b}", s_act)
                else:
                    p.emit("vector", lambda e, mc=mc, b=b: e.tensor_tensor(
                        out=shsv[:, mc, :], in0=pf[b][:],
                        in1=shsv[:, mc, :], op=ALU.mult),
                        wait=now(s_pe, s_act), inc=[(s_dve, 1)])
                    mark(f"shb_{b}", s_dve)
        for l4 in range(4):
            slot = l4 % 2
            p.emit("gpsimd", lambda e, l4=l4, slot=slot: e.dma_start(
                out=dprs[:, slot], in_=d_shd[l4]),
                wait=W(f"dpsl_{slot}"), inc=[(s_dp[slot], 16)])
            ldw = now(s_dp[slot])
            for hl in range(4):
                hm = l4 * 4 + hl
                b = hl % 2
                wv_ = (list(ldw) if hl == 0 else []) + W(f"sdb_{b}") + \
                      (now(s_dve, s_act) if l4 == 0 and hl == 0 else [])
                def dmm(e, slot=slot, hl=hl, tc=tc, b=b):
                    last = None
                    for ic in range(4):
                        last = e.matmul(pf[b][:], dprs[:, slot, hl, ic, :],
                                        shsv[:, ic, :],
                                        start=(ic == 0), stop=(ic == 3))
                    return last
                p.emit("tensor", dmm, wait=wv_, inc=[(s_pe, 1)])
                def xtr(e, hm=hm, tc=tc, b=b):
                    last = None
                    for u in range(4):
                        tt = tc * 4 + u
                        last = e.transpose(
                            pf[2 + b][:, u * 128:(u + 1) * 128],
                            x2[:, tt * HID + hm * 128: tt * HID + (hm + 1) * 128],
                            ident_f)
                    return last
                p.emit("tensor", xtr, wait=W(f"sdt_{b}"), inc=[(s_pe, 1)])
                p.emit("vector", lambda e, tc=tc, b=b: e.tensor_tensor(
                    out=oinsp[:], in0=pf[b][:],
                    in1=g1b[:, tc * 512:(tc + 1) * 512], op=ALU.mult),
                    wait=now(s_pe) + W("g1_done"))
                p.emit("vector", lambda e, hm=hm, tc=tc, b=b: e.tensor_tensor(
                    out=oav[:, hm, tc * 512:(tc + 1) * 512],
                    in0=pf[2 + b][:], in1=oinsp[:], op=ALU.add))
                mark(f"sdb_{b}", s_dve)
                mark(f"sdt_{b}", s_dve)
            mark(f"dpsl_{slot}", s_pe)
        for mc in range(4):
            mark(f"shsl_{mc}", s_pe)
        for _b in range(4):
            mark(f"shb_{_b}", s_dve)
    mark("oacc_init", s_dve)

    for m in range(8):
        b = m % 2
        wv_ = W(f"route_{b}") + (soft() if m == 0 else []) + \
              (now(s_one) if m == 0 else [])
        def lmm(e, m=m, b=b):
            last = None
            for c in range(KC):
                last = e.matmul(pf[b][:, 0:NE], r2v[:, c, m * 128:(m + 1) * 128],
                                gwv[:, c, :], start=(c == 0), stop=(c == KC - 1))
            return last
        p.emit("tensor", lmm, wait=wv_, inc=[(s_pe, 1)])
        def lcp(e, b=b):
            e.tensor_copy(out=ltile[:], in_=pf[b][:, 0:NE])
            return e.tensor_copy(out=cur[:], in_=pf[b][:, 0:NE])
        p.emit("vector", lcp, wait=now(s_pe, s_act))
        for r in range(TOPK):
            p.emit("vector", lambda e, r=r: e.reduce_max(mx[:, r:r + 1], cur[:],
                                                         axis=AXX))
            if r < TOPK - 1:
                p.emit("vector", lambda e, r=r: e.tensor_single_scalar(
                    out=mskt[:], in_=cur[:], scalar=mx[:, r:r + 1], op=ALU.is_ge))
                p.emit("vector", lambda e: e.scalar_tensor_tensor(
                    out=cur[:], in0=mskt[:], scalar=-BIG, in1=cur[:],
                    op0=ALU.mult, op1=ALU.add))
        p.emit("vector", lambda e: e.tensor_scalar(
            out=scal[:, 9:10], in0=mx[:, 0:1], scalar1=-1.0, scalar2=None,
            op0=ALU.mult))
        mark(f"route_{b}", s_dve)
        p.emit("scalar", lambda e: e.activation(out=expv[:], in_=ltile[:], func=AF.Exp,
                                                bias=scal[:, 9:10]),
               wait=now(s_dve), inc=[(s_act, 1)])
        p.emit("vector", lambda e: e.tensor_single_scalar(
            out=mskt[:], in_=ltile[:], scalar=mx[:, TOPK - 1:TOPK], op=ALU.is_ge),
            wait=now(s_act))
        p.emit("vector", lambda e: e.tensor_tensor(out=expv[:], in0=expv[:],
                                                   in1=mskt[:], op=ALU.mult))
        p.emit("vector", lambda e: e.reduce_sum(scal[:, 10:11], expv[:], axis=AXX))
        p.emit("vector", lambda e: e.reciprocal(out=scal[:, 10:11],
                                                in_=scal[:, 10:11]))
        p.emit("vector", lambda e: e.tensor_scalar(
            out=wtok[:], in0=expv[:], scalar1=scal[:, 10:11], scalar2=None,
            op0=ALU.mult))
        p.emit("tensor", lambda e: e.transpose(pf[6][0:NE, 0:128], wtok[:], ident_f),
               wait=now(s_dve), inc=[(s_pe, 1)])
        p.emit("vector", lambda e, m=m: e.tensor_copy(
            out=wfeat[0:NE, m * 128:(m + 1) * 128], in_=pf[6][0:NE, 0:128]),
            wait=now(s_pe), inc=[(s_dve, 1)])
    mark("wfeat_done", s_dve)

    for _b in range(4):
        mark(f"exb_{_b}", s_dve)
    mark("dpb_4", s_dve)
    mark("dpb_5", s_dve)
    mark("wbc_evac", s_dve)
    for e_ in range(ne_run):
        g = e_ // G
        eg = e_ % G
        wsl = e_ % 2
        p.emit("gpsimd", lambda e, e_=e_: e.dma_start(
            out=wrow[0:1, :], in_=wfeat[e_:e_ + 1, :]),
            wait=W("wfeat_done") + W(f"wk1_{e_ - 1}"), inc=[(s_wr, 16)])
        ldr = now(s_wr)
        for half in range(2):
            wv_ = (list(ldr) if half == 0 else []) + W("wbc_evac")
            p.emit("tensor", lambda e, half=half: e.matmul(
                pf[6][:], ones_row_b,
                wrow[0:1, half * 512:(half + 1) * 512],
                start=True, stop=True), wait=wv_, inc=[(s_pe, 1)])
            p.emit("scalar", lambda e, wsl=wsl, half=half: e.copy(
                out=wbc[:, wsl * TOK + half * 512: wsl * TOK + (half + 1) * 512],
                in_=pf[6][:]), wait=now(s_pe) +
                (W("shint_last") if e_ == 0 and half == 0 else []),
                inc=[(s_act, 1)])
            mark("wbc_evac", s_act)
        mark(f"wk1_{e_}", s_pe)
        if e_ == 0:
            pass

        for ps in range(4):
            for gi in range(2):
                m_ = ps + 4 * gi
                ci = e_ * 8 + ps * 2 + gi
                slot = ci % 3
                p.emit("gpsimd", lambda e, e_=e_, m_=m_, slot=slot: e.dma_start(
                    out=gurv[:, slot, :, :], in_=d_gu[e_, m_]),
                    wait=W(f"gusl_{slot}"), inc=[(s_gu[slot], 16)])
            for tc in range(2):
                cg = e_ * 8 + ps * 2
                sg_, su_ = cg % 3, (cg + 1) % 3
                for gi2, b, sl_ in ((0, tc, sg_), (1, 2 + tc, su_)):
                    wv_ = (now(s_gu[sl_]) if tc == 0 else []) + W(f"exb_{b}")
                    def gmm(e, sl_=sl_, tc=tc, b=b):
                        last = None
                        for c in range(KC):
                            last = e.matmul(pf[b][:], gurv[:, sl_, c, :],
                                            r2v[:, c, tc * 512:(tc + 1) * 512],
                                            start=(c == 0), stop=(c == KC - 1))
                        return last
                    p.emit("tensor", gmm, wait=wv_, inc=[(s_pe, 1)])
                mark(f"gusl_{sg_}", s_pe)
                mark(f"gusl_{su_}", s_pe)
                p.emit("scalar", lambda e, tc=tc: e.activation(
                    out=sgt[:, tc * 512:(tc + 1) * 512], in_=pf[tc][:], func=AF.Silu),
                    wait=now(s_pe) + W(f"sgt_{tc}"), inc=[(s_act, 1)])
                mark(f"exb_{tc}", s_act)
                p.emit("vector", lambda e, tc=tc: e.tensor_tensor(
                    out=itt[:, tc * 512:(tc + 1) * 512], in0=pf[2 + tc][:],
                    in1=sgt[:, tc * 512:(tc + 1) * 512], op=ALU.mult),
                    wait=now(s_act) + (W(f"dppe_{g - 1}")
                                       if eg == 0 and ps == 0 and tc == 0 else []))
                p.emit("vector", lambda e, tc=tc, eg=eg, ps=ps, wsl=wsl:
                       e.tensor_tensor(
                           out=interv[:, eg * 4 + ps, tc * 512:(tc + 1) * 512],
                           in0=itt[:, tc * 512:(tc + 1) * 512],
                           in1=wbc[:, wsl * TOK + tc * 512: wsl * TOK + (tc + 1) * 512],
                           op=ALU.mult))
                mark(f"exb_{2 + tc}", s_dve)
                mark(f"sgt_{tc}", s_dve)
        if e_ == 0:
            mark("shint_last_unused", s_dve)

        if eg == G - 1:
            mark(f"inter_{g}", s_dve)
            for hc in range(KC):
                dsl = hc % 2
                p.emit("gpsimd", lambda e, hc=hc, g=g, dsl=dsl: e.dma_start(
                    out=dprv[:, dsl], in_=d_dp[hc, :, g * G:(g + 1) * G]),
                    wait=W(f"dpsl_{dsl}"), inc=[(s_dp[dsl], 16)])
                ldw = now(s_dp[dsl])
                for tc in range(2):
                    b = 4 + (hc * 2 + tc) % 2
                    wv_ = W(f"inter_{g}") + (list(ldw) if tc == 0 else []) + \
                          W(f"dpb_{b}")
                    def pmm(e, dsl=dsl, tc=tc, b=b):
                        last = None
                        for ee in range(G):
                            for ic in range(4):
                                last = e.matmul(
                                    pf[b][:], dprv[:, dsl, ee, ic, :],
                                    interv[:, ee * 4 + ic, tc * 512:(tc + 1) * 512],
                                    start=(ee == 0 and ic == 0),
                                    stop=(ee == G - 1 and ic == 3))
                        return last
                    p.emit("tensor", pmm, wait=wv_, inc=[(s_pe, 1)])
                    p.emit("vector", lambda e, hc=hc, tc=tc, b=b: e.tensor_tensor(
                        out=oav[:, hc, tc * 512:(tc + 1) * 512], in0=pf[b][:],
                        in1=oav[:, hc, tc * 512:(tc + 1) * 512], op=ALU.add),
                        wait=now(s_pe), inc=[(s_dve, 1)])
                    mark(f"dpb_{b}", s_dve)
                mark(f"dpsl_{dsl}", s_pe)
            mark(f"dppe_{g}", s_pe)
    mark("moe_done", s_dve)

    if dumps:
        dbg = {"r2": r2, "wfeat": wfeat, "g1b": g1b, "wtok": wtok, "ltile": ltile, "mx": mx, "expv": expv, "cur": cur,
               "inter": inter, "wbc": wbc, "sqb": sqb, "x2d": x2}
        for i, name in enumerate(dumps):
            tsr = dbg[name]
            shp = [int(x) for x in tsr[:].shape]
            dt_ = tsr[:].dtype
            od = nc.dram_tensor(f"dump_{name}", shp, dt_, kind="ExternalOutput").ap()
            p.emit("sync", lambda e, od=od, tsr=tsr: e.dma_start(out=od, in_=tsr[:]),
                   wait=W("moe_done") + now(s_dve, s_act, s_pe), inc=[(s_st, 16)])

    for t in range(8):
        osl = t % 2
        for g4 in range(4):
            b = g4 % 2
            wv_ = (W("moe_done") if t == 0 and g4 == 0 else []) + W(f"otr_{b}")
            def otr(e, t=t, g4=g4, b=b):
                last = None
                for u in range(4):
                    gg = g4 * 4 + u
                    last = e.transpose(pf[b][:, u * 128:(u + 1) * 128],
                                       oav[:, gg, t * 128:(t + 1) * 128], ident_f)
                return last
            p.emit("tensor", otr, wait=wv_, inc=[(s_pe, 1)])
            p.emit("vector", lambda e, g4=g4, b=b, osl=osl: e.tensor_copy(
                out=outstg[:, osl * HID + g4 * 512: osl * HID + (g4 + 1) * 512],
                in_=pf[b][:]),
                wait=now(s_pe) + ([(s_so[osl], 16 * (t // 2))] if t >= 2 and g4 == 0 else []),
                inc=[(s_dve, 1)])
            mark(f"otr_{b}", s_dve)
        p.emit("sync", lambda e, t=t, osl=osl: e.dma_start(
            out=d_out[t * 128:(t + 1) * 128, :],
            in_=outstg[:, osl * HID:(osl + 1) * HID]),
            wait=now(s_dve), inc=[(s_so[osl], 16)])
    p.emit("sync", lambda e, v0=s_so[0].val, v1=s_so[1].val:
           (e.wait_ge(s_so[0].h, v0), e.wait_ge(s_so[1].h, v1))[-1])

    return nc, p, dram


def _consts():
    cbv = np.zeros((128, 256), ml_dtypes.bfloat16)
    cbv[:, 0:128] = np.eye(128, dtype=ml_dtypes.bfloat16)
    cbv[:, 128:256] = 1
    cfv = np.zeros((128, 772), np.float32)
    cfv[:, 0:128] = np.eye(128, dtype=np.float32)
    cfv[:, 128:256] = 1
    cfv[:, 256:768] = (np.arange(512, dtype=np.float32)[None, :]
                       - np.arange(128, dtype=np.float32)[:, None])
    cfv[:, 768:769] = 1e-5
    return cbv, cfv


def _tau_for_core(s):
    tau = np.zeros(N_TAU, np.float32)
    for (qc, kt, ti, scj) in SCORE_CHUNKS:
        if ti is not None:
            tau[ti] = kt * 128 - qc * 512
    return np.ascontiguousarray(np.broadcast_to(tau[None, :], (128, N_TAU)))


def _ksc_for_core(s):
    v = np.full(N_KSC, 0.0 if s == 0 else 1.0, np.float32)
    return np.ascontiguousarray(np.broadcast_to(v[None, :], (128, N_KSC)))


def prep_weights(inputs):
    d = {}
    def qkvperm(w):
        return np.ascontiguousarray(
            np.asarray(w, np.float32).reshape(16, 128, 16, 128).transpose(0, 3, 2, 1))
    d["wq"] = qkvperm(inputs["L0_attn_Wq"])
    d["wk"] = qkvperm(inputs["L0_attn_Wk"])
    d["wv"] = qkvperm(inputs["L0_attn_Wv"])
    d["wo"] = np.ascontiguousarray(
        np.asarray(inputs["L0_attn_Wo"], np.float32).reshape(2048, 16, 128)
        .transpose(2, 1, 0))
    d["gamma"] = np.ascontiguousarray(
        np.asarray(inputs["L0_attn_gamma"], np.float32).reshape(128, 1))
    d["beta"] = np.ascontiguousarray(
        np.asarray(inputs["L0_attn_beta"], np.float32).reshape(128, 1))
    gu = np.asarray(inputs["L0_ffn_gu_w"], np.float32)
    d["gu"] = np.ascontiguousarray(
        gu.reshape(64, 8, 128, 16, 128).transpose(0, 1, 4, 3, 2))
    dp = np.asarray(inputs["L0_ffn_dp_w"], np.float32)
    d["dp"] = np.ascontiguousarray(
        dp.reshape(64, 16, 128, 4, 128).transpose(1, 4, 0, 3, 2))
    d["gatew"] = np.ascontiguousarray(
        np.asarray(inputs["L0_ffn_gate_w"], np.float32).reshape(64, 16, 128)
        .transpose(2, 1, 0))
    def shperm(w):
        return np.asarray(w, np.float32).reshape(4, 128, 16, 128).transpose(0, 3, 2, 1)
    d["shg"] = np.ascontiguousarray(np.stack(
        [shperm(inputs["L0_ffn_sh_gate_w"]), shperm(inputs["L0_ffn_sh_up_w"])], axis=0))
    d["shd"] = np.ascontiguousarray(
        np.asarray(inputs["L0_ffn_sh_down_w"], np.float32)
        .reshape(4, 4, 128, 4, 128).transpose(0, 4, 1, 3, 2))
    d["shg1"] = np.ascontiguousarray(
        np.asarray(inputs["L0_ffn_sh_gate1_w"], np.float32).reshape(16, 128)
        .transpose(1, 0)).reshape(128, 16, 1)
    d["consts_b"], d["consts_f"] = _consts()
    return d


def kernel(**inputs):
    x = np.asarray(inputs["L0_attn_hidden_states"], np.float32)
    shared = prep_weights(inputs)
    nc, p, dram = build_program("full")
    p.finish()
    in_maps = []
    for c in range(8):
        b, s = c // 2, c % 2
        own = x[b, s * TOK:(s + 1) * TOK]
        other = x[b, (1 - s) * TOK:(2 - s) * TOK]
        m = dict(shared)
        m["x_seq"] = np.ascontiguousarray(np.concatenate([own, other], axis=0))
        m["x_own"] = np.ascontiguousarray(own)
        m["tau"] = _tau_for_core(s)
        m["ksc"] = _ksc_for_core(s)
        in_maps.append(m)
    res = run_bass_kernel_spmd(nc, in_maps, core_ids=list(range(8)))
    out = np.empty((4, 2048, 2048), np.float32)
    for c in range(8):
        b, s = c // 2, c % 2
        out[b, s * TOK:(s + 1) * TOK] = res.results[c]["out"]
    return out
